# version 06
# baseline (speedup 1.0000x reference)
import contextlib
import numpy as np
import ml_dtypes
import concourse.bass as bass
import concourse.mybir as mybir
from concourse.bass_utils import run_bass_kernel_spmd

F32 = mybir.dt.float32
BF16 = mybir.dt.bfloat16
FP8 = mybir.dt.float8e4
AF = mybir.ActivationFunctionType
ALU = mybir.AluOpType
AX = mybir.AxisListType

D = 1024
L = 2048
BLOC = 2
T = BLOC * L
NCH = D // 128
HFF = 4096
HCH = HFF // 128
N_SUBLAYERS = 4

VKB = 16
VHEAD = 65
VROW = 16 * VHEAD


class Builder:
    SKIP_NORM = False
    DUMP_RB = False
    FFN_NO_LN2 = False
    FFN_NO_STATS = False
    FFN_LN_PSUM_ALT = False
    FFN_SERIAL = False
    FFN_BF16_W = False
    QUANT_SERIAL = False
    W8_EXTOUT = False

    def __init__(self, n_sublayers=N_SUBLAYERS, dbg=False, rd=False):
        self.nc = bass.Bass("TRN2", target_bir_lowering=False,
                            detect_race_conditions=rd)
        self.nsub = n_sublayers
        self.dbg = dbg
        self.x_update_v = 0
        self.av_free_v = 0
        self._prev_exp15_s = 0
        self._prev_av_tail = {}
        self._pr_evac_v = 0
        self._pr_rope_v = 0
        self._pr_rot_used_v = 0
        self._outproj_read_t = 0
        self._wo_used_t = {0: 0, 1: 0}
        self._ybuf_read_t = 0
        self._qg_read_t = 0
        self._rb_t = 0
        self._aost_used_t = 0
        self._x2_scratch_t = 0

    def _init_sems(self, stk):
        self.sems, self.cnt, self.mark = {}, {}, {}
        for s in ("dg", "ds", "da", "t", "s", "v"):
            self.sems[s] = stk.enter_context(self.nc.semaphore(s))
            self.cnt[s] = 0

    def nm(self, base):
        self._uid = getattr(self, "_uid", 0) + 1
        return f"{base}_{self._uid}"

    def inc(self, inst, sem, n=1):
        inst.then_inc(self.sems[sem], n)
        self.cnt[sem] += n
        return self.cnt[sem]

    def wait(self, ename, eng, sem, val):
        if val <= 0:
            return
        assert val <= self.cnt[sem], (sem, val, self.cnt[sem])
        key = (ename, sem)
        if self.mark.get(key, 0) >= val:
            return
        eng.wait_ge(self.sems[sem], val)
        self.mark[key] = val

    def tbody(self, fn):
        self.block.tensor(fn)

    def vbody(self, fn):
        self.block.vector(fn)

    def sbody(self, fn):
        self.block.scalar(fn)

    def gbody(self, fn):
        self.block.gpsimd(fn)

    def ybody(self, fn):
        self.block.sync(fn)

    def build(self):
        nc = self.nc
        dt = nc.dram_tensor
        self.xT = dt("xT", [D, T], F32, kind="ExternalInput").ap()
        self.wqkvT = dt("wqkvT", [D, 3 * D], F32, kind="ExternalInput").ap()
        self.woutT = dt("woutT", [D, D], F32, kind="ExternalInput").ap()
        self.wqT = dt("wqT", [D, D], F32, kind="ExternalInput").ap()
        self.wkT = dt("wkT", [D, D], F32, kind="ExternalInput").ap()
        self.wvT = dt("wvT", [D, D], F32, kind="ExternalInput").ap()
        self.woT = dt("woT", [D, D], F32, kind="ExternalInput").ap()
        self.wfcT = dt("wfcT", [D, HFF], F32, kind="ExternalInput").ap()
        self.wprojT = dt("wprojT", [HFF, D], F32, kind="ExternalInput").ap()
        self.lnwb = dt("lnwb", [128, 64], F32, kind="ExternalInput").ap()
        self.cosrep = dt("cosrep", [128, L], F32, kind="ExternalInput").ap()
        self.sinrep = dt("sinrep", [128, L], F32, kind="ExternalInput").ap()
        self.p32 = dt("p32", [128, 128], F32, kind="ExternalInput").ap()
        self.diagm = dt("diagm", [128, 128], F32, kind="ExternalInput").ap()
        self.ident = dt("ident", [128, 128], F32, kind="ExternalInput").ap()
        self.p64m = dt("p64m", [128, 128], F32, kind="ExternalInput").ap()
        self.out = dt("out", [D, T], F32, kind="ExternalOutput").ap()
        w8kind = "ExternalOutput" if (self.dbg or Builder.W8_EXTOUT) \
            else "Internal"
        self.wfc8h = dt("wfc8h", [D, HFF], FP8, kind=w8kind).ap()
        self.wproj8h = dt("wproj8h", [HFF, D], FP8, kind=w8kind).ap()
        if self.dbg:
            self.d_y = dt("d_y", [128, NCH * L], F32, kind="ExternalOutput").ap()
            self.d_v = dt("d_v", [128, VKB * VROW], F32, kind="ExternalOutput").ap()
            self.d_qg = dt("d_qg", [128, 2 * L], F32, kind="ExternalOutput").ap()
            self.d_kg = dt("d_kg", [128, 2 * L], F32, kind="ExternalOutput").ap()
            self.d_ao = dt("d_ao", [128, 2 * L], F32, kind="ExternalOutput").ap()
            self.d_e = dt("d_e", [128, 2 * L], F32, kind="ExternalOutput").ap()
            self.d_rb = dt("d_rb", [128, 1536], F32, kind="ExternalOutput").ap()
            self.d_h = dt("d_h", [128, HCH * 512], F32, kind="ExternalOutput").ap()
            self.d_yf = dt("d_yf", [128, NCH * 512], F32, kind="ExternalOutput").ap()
            self.d_x = dt("d_x", [128, NCH * T], F32, kind="ExternalOutput").ap()
            self.d_cs = dt("d_cs", [2, 128, 80], F32, kind="ExternalOutput").ap()

        with contextlib.ExitStack() as stk:
            A = stk.enter_context
            sb = nc.sbuf_tensor
            self.x_sb = A(sb("x_sb", [128, NCH * T], BF16))
            self.cos_sb = A(sb("cos_sb", [128, L], BF16))
            self.sin_sb = A(sb("sin_sb", [128, L], BF16))
            self.p32_sb = A(sb("p32_sb", [128, 128], BF16))
            self.diag_sb = A(sb("diag_sb", [128, 128], BF16))
            self.p64_sb = A(sb("p64_sb", [128, 128], BF16))
            self.ones_bf = A(sb("ones_bf", [128, 1], BF16))
            self.ones_f = A(sb("ones_f", [128, 128], F32))
            self.rowbuf = A(sb("rowbuf", [128, 1536], F32))
            self.lnwb_sb = A(sb("lnwb_sb", [128, 64], F32))
            self.x2buf = A(sb("x2buf", [128, 4 * 512], BF16))
            self.drain_sb = A(sb("drain_sb", [128, 1], F32))
            self._init_sems(stk)
            self.block = A(nc.Block())

            self.phase_init()
            if self.nsub >= 2 or self.nsub in (-1, -2, -3):
                self.quant_phase()
            if self.nsub >= 1 or self.nsub == -1:
                self.attn_sublayer("a")
            if self.nsub >= 2 or self.nsub == -2:
                self.ffn_sublayer(final=False)
            if self.nsub >= 3:
                self.attn_sublayer("b")
            if self.nsub >= 4:
                self.ffn_sublayer(final=True)
            self.phase_final(direct_out=(self.nsub >= 4))

        return nc

    def phase_init(self):
        K = self

        def g(g):
            for c in range(NCH):
                K.inc(g.dma_start(out=K.x_sb[:, c * T:(c + 1) * T],
                                  in_=K.xT[c * 128:(c + 1) * 128, :]), "dg", 16)
            K.inc(g.dma_start(out=K.cos_sb[:], in_=K.cosrep), "dg", 16)
            K.inc(g.dma_start(out=K.sin_sb[:], in_=K.sinrep), "dg", 16)
            K.inc(g.dma_start(out=K.p32_sb[:], in_=K.p32), "dg", 16)
            K.inc(g.dma_start(out=K.diag_sb[:], in_=K.diagm), "dg", 16)
            K.inc(g.dma_start(out=K.p64_sb[:], in_=K.p64m), "dg", 16)
            K.inc(g.dma_start(out=K.lnwb_sb[:], in_=K.lnwb), "dg", 16)
        self.gbody(g)
        self.x_dg = self.dma_flush()

        def v(v):
            v.memset(K.ones_bf[:], 1.0)
            K.inc(v.memset(K.ones_f[:], 1.0), "v", 1)
            K.wait("vector", v, "dg", self.x_dg)
            K.inc(v.tensor_scalar_mul(K.lnwb_sb[:], K.lnwb_sb[:], 0.5),
                  "v", 1)
        self.vbody(v)

    def dma_flush(self):
        K = self

        def g(g):
            K.inc(g.dma_start(out=self.drain_sb[0:1, 0:1],
                              in_=self.p32[0:1, 0:1]), "dg", 16)
        self.gbody(g)
        return K.cnt["dg"]

    def dump(self, dram_ap, sbuf_ap, vwait=None, twait=None, swait=None):
        if not self.dbg:
            return 0
        K = self

        def g(g):
            if vwait:
                K.wait("gpsimd", g, "v", vwait)
            if twait:
                K.wait("gpsimd", g, "t", twait)
            if swait:
                K.wait("gpsimd", g, "s", swait)
            K.inc(g.dma_start(out=dram_ap, in_=sbuf_ap), "dg", 16)
        self.gbody(g)
        return K.cnt["dg"]

    def _x_waits(self):
        w = [("dg", self.x_dg)]
        if self.x_update_v:
            w.append(("v", self.x_update_v))
        return w

    def sublayer_barrier(self):
        K = self
        snap = dict(K.cnt)

        def mk(ename):
            def f(e):
                for sem in ("t", "v", "s", "dg", "ds"):
                    if not (ename == "tensor" and sem == "t") \
                            and not (ename == "vector" and sem == "v") \
                            and not (ename == "scalar" and sem == "s"):
                        K.wait(ename, e, sem, snap[sem])
            return f
        self.tbody(mk("tensor"))
        self.vbody(mk("vector"))
        self.sbody(mk("scalar"))
        self.gbody(mk("gpsimd"))
        self.ybody(mk("sync"))

    def ln_strip(self, t0, ydst, ps_sx, ps_sxx, ps_rb, ps_mrb,
                 xwaits=(), twaits=(), ywaits=(), half=False):
        K = self
        eps = np.float32(1e-5)
        rb = K.rowbuf
        rnd_t = {}
        for r in range(4):
            c0 = 2 * r

            def v1(v, r=r, c0=c0):
                if r == 0:
                    for sem, val in xwaits:
                        K.wait("vector", v, sem, val)
                if r >= 2:
                    K.wait("vector", v, "t", rnd_t[r - 2])
                for c in (c0, c0 + 1):
                    xs = K.x_sb[:, c * T + t0: c * T + t0 + 512]
                    K.inc(v.tensor_tensor(
                        K.x2buf[:, (c % 4) * 512:(c % 4 + 1) * 512],
                        xs, xs, op=ALU.mult), "v", 1)
            self.vbody(v1)
            x2_v = K.cnt["v"]

            def t1(t, r=r, c0=c0, x2_v=x2_v):
                if r == 0:
                    for sem, val in xwaits:
                        K.wait("tensor", t, sem, val)
                    for sem, val in twaits:
                        K.wait("tensor", t, sem, val)
                K.wait("tensor", t, "v", x2_v)
                for c in (c0, c0 + 1):
                    xs = K.x_sb[:, c * T + t0: c * T + t0 + 512]
                    t.matmul(ps_sx[0:1, :], K.ones_bf[:], xs,
                             start=(c == 0), stop=(c == NCH - 1))
                    i = t.matmul(ps_sxx[0:1, :], K.ones_bf[:],
                                 K.x2buf[:, (c % 4) * 512:(c % 4 + 1) * 512],
                                 start=(c == 0), stop=(c == NCH - 1))
                K.inc(i, "t", 1)
            self.tbody(t1)
            rnd_t[r] = K.cnt["t"]
        stats_t = K.cnt["t"]

        def v2(v):
            K.wait("vector", v, "t", stats_t)
            K.inc(v.tensor_scalar_mul(rb[0:1, 0:512], ps_sx[0:1, :],
                                      1.0 / D), "v", 1)
            K.inc(v.tensor_tensor(rb[0:1, 512:1024], rb[0:1, 0:512],
                                  rb[0:1, 0:512], op=ALU.mult), "v", 1)
            K.inc(v.scalar_tensor_tensor(rb[0:1, 512:1024], ps_sxx[0:1, :],
                                         1.0 / D, rb[0:1, 512:1024],
                                         op0=ALU.mult, op1=ALU.subtract), "v", 1)
        self.vbody(v2)
        var_v = K.cnt["v"]

        def s2(s):
            K.wait("scalar", s, "v", var_v)
            K.inc(s.activation(rb[0:1, 512:1024], rb[0:1, 512:1024],
                               AF.Ln, bias=eps), "s", 1)
            K.inc(s.activation(rb[0:1, 1024:1536], rb[0:1, 512:1024],
                               AF.Exp, scale=np.float32(-0.5),
                               bias=np.float32(-0.6931471805599453) if half
                               else np.float32(0.0)), "s", 1)
        self.sbody(s2)
        rstd_s = K.cnt["s"]

        def v3(v):
            K.wait("vector", v, "s", rstd_s)
            K.inc(v.tensor_tensor(rb[0:1, 0:512], rb[0:1, 0:512],
                                  rb[0:1, 1024:1536], op=ALU.mult), "v", 1)
        self.vbody(v3)
        mur_v = K.cnt["v"]

        def t2(t):
            K.wait("tensor", t, "v", mur_v)
            t.matmul(ps_rb, K.ones_f[0:1, :], rb[0:1, 1024:1536],
                     start=True, stop=True)
            i = t.matmul(ps_mrb, K.ones_f[0:1, :], rb[0:1, 0:512],
                         start=True, stop=True)
            K.inc(i, "t", 1)
        self.tbody(t2)
        bc_t = K.cnt["t"]

        def v4(v):
            K.wait("vector", v, "t", bc_t)
            for sem, val in twaits:
                K.wait("vector", v, sem, val)
            for sem, val in ywaits:
                K.wait("vector", v, sem, val)
            for c in range(NCH):
                xs = K.x_sb[:, c * T + t0: c * T + t0 + 512]
                K.inc(v.tensor_tensor(K.rowbuf[:, 512:1024], xs, ps_rb,
                                      op=ALU.mult), "v", 1)
                K.inc(v.scalar_tensor_tensor(ydst(c), ps_mrb, -1.0,
                                             K.rowbuf[:, 512:1024],
                                             op0=ALU.mult,
                                             op1=ALU.add), "v", 1)
        self.vbody(v4)
        return K.cnt["v"]

    def quant_phase(self):
        K = self
        nc = self.nc
        with contextlib.ExitStack() as stk:
            A = stk.enter_context
            wst = A(nc.sbuf_tensor(K.nm("q_wst"), [128, 4 * 512], F32))
            qst = A(nc.sbuf_tensor(K.nm("q_qst"), [128, 4 * 512], BF16))
            sgn = A(nc.sbuf_tensor(K.nm("q_sgn"), [128, 4 * 512], BF16))
            colst = A(nc.sbuf_tensor(K.nm("q_colst"), [128, 80], F32))
            ps_g = A(nc.psum_tensor(K.nm("q_psg"), [128, 512], F32))

            for wi in range(2):
                wsrc = self.wfcT if wi == 0 else self.wprojT
                w8dst = self.wfc8h if wi == 0 else self.wproj8h
                nrow_ch = (D if wi == 0 else HFF) // 128
                ncol_ch = (HFF if wi == 0 else D) // 512
                chunks = [(r, cc) for r in range(nrow_ch)
                          for cc in range(ncol_ch)]
                nchunks = len(chunks)
                assert nchunks == 64
                ngrp = nchunks // 2

                s_prev = K.cnt["s"]
                red_v = {}
                for g in range(ngrp):
                    def gw1(gq, g=g):
                        K.wait("gpsimd", gq, "dg", K.cnt["dg"])
                        if g >= 2:
                            K.wait("gpsimd", gq, "v", red_v[g - 2])
                        else:
                            K.wait("gpsimd", gq, "s", s_prev)
                        for c in (2 * g, 2 * g + 1):
                            r, cc = chunks[c]
                            K.inc(gq.dma_start(
                                out=wst[:, (c % 4) * 512:(c % 4 + 1) * 512],
                                in_=wsrc[r * 128:(r + 1) * 128,
                                         cc * 512:(cc + 1) * 512]), "dg", 16)
                    self.gbody(gw1)
                    gw_dg = K.cnt["dg"]

                    def vr(v, g=g, gw_dg=gw_dg):
                        K.wait("vector", v, "dg", gw_dg)
                        for c in (2 * g, 2 * g + 1):
                            K.inc(v.reduce_sum(
                                colst[:, c:c + 1],
                                wst[:, (c % 4) * 512:(c % 4 + 1) * 512],
                                axis=AX.X, apply_absolute_value=True), "v", 1)
                    self.vbody(vr)
                    red_v[g] = K.cnt["v"]
                p1_v = K.cnt["v"]

                def v_tot(v):
                    K.inc(v.reduce_sum(colst[:, 77:78], colst[:, 0:nchunks],
                                       axis=AX.X), "v", 1)
                self.vbody(v_tot)
                tot_v = K.cnt["v"]

                def t_g(t):
                    K.wait("tensor", t, "v", tot_v)
                    K.inc(t.matmul(ps_g[0:1, 0:1], colst[:, 77:78],
                                   self.ones_f[:, 0:1], start=True, stop=True),
                          "t", 1)
                self.tbody(t_g)
                g_t = K.cnt["t"]
                nel = float(D * HFF)

                def v_g(v):
                    K.wait("vector", v, "t", g_t)
                    K.inc(v.tensor_scalar(K.rowbuf[0:1, 0:1], ps_g[0:1, 0:1],
                                          0.5 / nel, 0.5e-6,
                                          op0=ALU.mult, op1=ALU.add), "v", 1)
                    K.inc(v.tensor_scalar(K.rowbuf[0:1, 1:2], ps_g[0:1, 0:1],
                                          -0.5 / nel, -0.5e-6,
                                          op0=ALU.mult, op1=ALU.add), "v", 1)
                self.vbody(v_g)
                th_s = K.cnt["v"]

                def t_b(t):
                    K.wait("tensor", t, "v", th_s)
                    t.matmul(ps_g[:, 0:1], self.ones_f[0:1, :],
                             K.rowbuf[0:1, 0:1], start=True, stop=True)
                    K.inc(t.matmul(ps_g[:, 1:2], self.ones_f[0:1, :],
                                   K.rowbuf[0:1, 1:2], start=True, stop=True),
                          "t", 1)
                self.tbody(t_b)
                thb_t = K.cnt["t"]
                s_prev2 = K.cnt["s"]

                def v_th(v):
                    K.wait("vector", v, "t", thb_t)
                    K.wait("vector", v, "s", s_prev2)
                    K.inc(v.tensor_copy(colst[:, 78:80], ps_g[:, 0:2]), "v", 1)
                self.vbody(v_th)
                th_v = K.cnt["v"]

                if self.dbg:
                    self.dump(getattr(self, "d_cs", None)[wi], colst[:],
                              vwait=K.cnt["v"])

                dg2s = K.cnt["dg"]
                sgn_s = {}
                qv = {}
                sgrp_w = {}
                for g in range(ngrp):
                    def gw2(gq, g=g):
                        K.wait("gpsimd", gq, "dg", K.cnt["dg"])
                        if g >= 2:
                            K.wait("gpsimd", gq, "s", sgn_s[g - 2])
                        else:
                            K.wait("gpsimd", gq, "v", p1_v)
                        for c in (2 * g, 2 * g + 1):
                            r, cc = chunks[c]
                            K.inc(gq.dma_start(
                                out=wst[:, (c % 4) * 512:(c % 4 + 1) * 512],
                                in_=wsrc[r * 128:(r + 1) * 128,
                                         cc * 512:(cc + 1) * 512]), "dg", 16)
                    self.gbody(gw2)
                    l_dg = K.cnt["dg"]

                    def s2(s, g=g, l_dg=l_dg):
                        K.wait("scalar", s, "dg", l_dg)
                        K.wait("scalar", s, "v", th_v)
                        if g >= 1:
                            K.wait("scalar", s, "v", qv[g - 1])
                        for c in (2 * g, 2 * g + 1):
                            ws = wst[:, (c % 4) * 512:(c % 4 + 1) * 512]
                            K.inc(s.sign(sgn[:, (c % 4) * 512:
                                             (c % 4 + 1) * 512],
                                         ws, bias=colst[:, 78:79]), "s", 1)
                            K.inc(s.sign(K.rowbuf[:, (c % 2) * 512:
                                                  (c % 2) * 512 + 512],
                                         ws, bias=colst[:, 79:80]), "s", 1)
                    self.sbody(s2)
                    sgn_s[g] = K.cnt["s"]

                    def v2(v, g=g):
                        K.wait("vector", v, "s", sgn_s[g])
                        if g >= 2:
                            K.wait("vector", v, "dg", sgrp_w[g - 2])
                        else:
                            K.wait("vector", v, "dg", dg2s)
                        for c in (2 * g, 2 * g + 1):
                            K.inc(v.tensor_tensor(
                                qst[:, (c % 4) * 512:(c % 4 + 1) * 512],
                                sgn[:, (c % 4) * 512:(c % 4 + 1) * 512],
                                K.rowbuf[:, (c % 2) * 512:(c % 2) * 512 + 512],
                                op=ALU.add), "v", 1)
                    self.vbody(v2)
                    qv[g] = K.cnt["v"]

                    def gw3(gq, g=g):
                        K.wait("gpsimd", gq, "dg", K.cnt["dg"])
                        K.wait("gpsimd", gq, "v", qv[g])
                        for c in (2 * g, 2 * g + 1):
                            r, cc = chunks[c]
                            K.inc(gq.dma_start(
                                out=w8dst[r * 128:(r + 1) * 128,
                                          cc * 512:(cc + 1) * 512],
                                in_=qst[:, (c % 4) * 512:(c % 4 + 1) * 512]),
                                "dg", 16)
                    self.gbody(gw3)
                    sgrp_w[g] = K.cnt["dg"]
        self.quant_dg = K.cnt["dg"]

    def attn_sublayer(self, variant):
        K = self
        nc = self.nc
        causal = (variant == "b")
        self.sublayer_barrier()
        with contextlib.ExitStack() as stk:
            A = stk.enter_context
            sb = nc.sbuf_tensor
            ybuf = A(sb(K.nm("a_y"), [128, NCH * L], BF16))
            v_sb = A(sb(K.nm("a_v"), [128, VKB * VROW], BF16))
            ao_stage = A(sb(K.nm("a_aost"), [128, 1024], BF16))
            qg = A(sb(K.nm("a_qg"), [128, 4 * L], BF16))
            kg = A(sb(K.nm("a_kg"), [128, 2 * L], BF16))
            qsc = self.x2buf
            ao = A(sb(K.nm("a_ao"), [128, 2 * L], BF16))
            e_sb = A(sb(K.nm("a_e"), [128, 2 * L], BF16))
            arena = A(sb(K.nm("a_arena"), [128, 8192], BF16))

            def vz(v):
                ap = v_sb[:].rearrange("p (k c) -> p k c", c=VHEAD)
                v.memset(ap[:, :, 64:65], 1.0)
                v.memset(ao_stage[64:128, :], 0.0)
                for pl in range(2):
                    v.memset(qg[64:128, (2 * pl) * L:(2 * pl + 1) * L], 0.0)
                    v.memset(qg[0:64, (2 * pl + 1) * L:(2 * pl + 2) * L], 0.0)
                K.inc(v.memset(self.drain_sb[:], 0.0), "v", 1)
            self.vbody(vz)
            self._vz_v = K.cnt["v"]

            self._slot_read_t = {0: 0, 1: 0, 2: 0}
            self._vproj_done_t = 0
            self._qsc = qsc
            xupd = []
            for b in range(BLOC):
                self._attn_batch(variant, causal, b, ybuf, v_sb, qg, kg, ao,
                                 e_sb, arena, ao_stage, xupd)
            self.x_update_v = max(xupd)

    def _attn_batch(self, variant, causal, b, ybuf, v_sb, qg, kg, ao, e_sb,
                    arena, ao_stage, xupd):
        K = self
        nc = self.nc
        tb = b * L

        with contextlib.ExitStack() as stk:
            A = stk.enter_context
            ps_sx = A(nc.psum_tensor(K.nm("ps_sx"), [128, 512], F32))
            ps_sxx = A(nc.psum_tensor(K.nm("ps_sxx"), [128, 512], F32))
            ps_rb = A(nc.psum_tensor(K.nm("ps_rb"), [128, 512], F32))
            ps_mrb = A(nc.psum_tensor(K.nm("ps_mrb"), [128, 512], F32))
            ps_v = [A(nc.psum_tensor(K.nm(f"ps_v{i}"), [128, 1024], F32))
                    for i in range(2)]

            ywaits = []
            if b == 1:
                ywaits.append(("t", self._ybuf_read_t))
            for sx in range(4):
                self.ln_strip(
                    tb + sx * 512,
                    lambda c, sx=sx: ybuf[:, c * L + sx * 512:
                                          c * L + sx * 512 + 512],
                    ps_sx, ps_sxx, ps_rb[:], ps_mrb[:],
                    xwaits=self._x_waits(), ywaits=ywaits)
            y_v = K.cnt["v"]
            self._y_ready_v = y_v
            if b == 0 and not getattr(self, "_dumped_y", False):
                self._dumped_y = True
                self.dump(getattr(self, "d_y", None), ybuf[:], vwait=y_v)

            def gw(g):
                K.wait("gpsimd", g, "dg", K.cnt["dg"])
                K.wait("gpsimd", g, "t", K.cnt["t"])
                wsrc = (self.wqkvT[:, 2 * D:3 * D] if variant == "a"
                        else self.wvT)
                for c in range(NCH):
                    K.inc(g.dma_start(out=arena[:, c * 1024:(c + 1) * 1024],
                                      in_=wsrc[c * 128:(c + 1) * 128, :]),
                          "dg", 16)
            self.gbody(gw)
            wv_dg = self.dma_flush()

            evac_v = {}
            for tt in range(16):
                def t_f(t, tt=tt):
                    if tt == 0:
                        K.wait("tensor", t, "dg", wv_dg)
                        K.wait("tensor", t, "v", max(y_v, self._vz_v))
                    if tt >= 2:
                        K.wait("tensor", t, "v", evac_v[tt - 2])
                    for c in range(NCH):
                        ysl = ybuf[:, c * L + tt * 128: c * L + tt * 128 + 128]
                        for eh in range(2):
                            i = t.matmul(
                                ps_v[tt % 2][:, eh * 512:(eh + 1) * 512], ysl,
                                arena[:, c * 1024 + eh * 512:
                                      c * 1024 + (eh + 1) * 512],
                                start=(c == 0), stop=(c == NCH - 1))
                    K.inc(i, "t", 1)
                self.tbody(t_f)
                vp_t = K.cnt["t"]

                def v_f(v, tt=tt, vp_t=vp_t):
                    K.wait("vector", v, "t", vp_t)
                    for eh in range(2):
                        src = ps_v[tt % 2][:, eh * 512:(eh + 1) * 512]\
                            .rearrange("p (h e) -> p h e", e=64)
                        base = tt * VROW + 8 * eh * VHEAD
                        dst = v_sb[:, base: base + 8 * VHEAD].rearrange(
                            "p (h c) -> p h c", c=VHEAD)
                        K.inc(v.tensor_copy(dst[:, :, 0:64], src), "v", 1)
                    evac_v[tt] = K.cnt["v"]
                self.vbody(v_f)

        self._vproj_done_t = self.cnt["t"]
        if b == 0 and not getattr(self, "_dumped_v", False):
            self._dumped_v = True
            self.dump(getattr(self, "d_v", None), v_sb[:],
                      vwait=self.cnt["v"])
        for grp in range(4):
            self._attn_group(variant, causal, b, grp, ybuf, v_sb, qg, kg, ao,
                             e_sb, arena, ao_stage, xupd)
        self._ybuf_read_t = self.cnt["t"]

    def _attn_group(self, variant, causal, b, grp, ybuf, v_sb, qg, kg, ao,
                    e_sb, arena, ao_stage, xupd):
        K = self
        nc = self.nc
        slot_base = {0: 4096, 1: 0, 2: 2048}
        pair_slot = {}

        for pl in range(2):
            hp = 2 * grp + pl
            p_glob = hp
            slot = p_glob % 3
            pair_slot[pl] = slot

            def gw(g, hp=hp, slot=slot):
                K.wait("gpsimd", g, "dg", K.cnt["dg"])
                K.wait("gpsimd", g, "t", self._slot_read_t[slot])
                K.wait("gpsimd", g, "t", self._vproj_done_t)
                if variant == "a":
                    qsrc = self.wqkvT[:, hp * 128:(hp + 1) * 128]
                    ksrc = self.wqkvT[:, D + hp * 128: D + (hp + 1) * 128]
                else:
                    qsrc = self.wqT[:, hp * 128:(hp + 1) * 128]
                    ksrc = self.wkT[:, hp * 128:(hp + 1) * 128]
                dst = arena[:, slot_base[slot]:slot_base[slot] + 2048]\
                    .rearrange("p (c two e) -> p c two e", two=2, e=128)
                K.inc(g.dma_start(out=dst[:, :, 0, :],
                                  in_=qsrc.rearrange("(c p) e -> p c e",
                                                     p=128)), "dg", 16)
                K.inc(g.dma_start(out=dst[:, :, 1, :],
                                  in_=ksrc.rearrange("(c p) e -> p c e",
                                                     p=128)), "dg", 16)
            self.gbody(gw)
        wqk_dg = self.dma_flush()

        with contextlib.ExitStack() as stk:
            A = stk.enter_context
            ps_qk = A(nc.psum_tensor(K.nm("ps_qk"), [128, 512], F32))
            ps_rot = A(nc.psum_tensor(K.nm("ps_rot"), [128, 512], F32))
            for pl in range(2):
                wqk_ap = arena[:, slot_base[pair_slot[pl]]:
                               slot_base[pair_slot[pl]] + 2048]
                for qk, dst in ((0, qg), (1, kg)):
                    for sx in range(4):
                        self._proj_rope_strip(b, pl, qk, sx, ybuf, wqk_ap,
                                              dst, ps_qk, ps_rot, wqk_dg)
        rope_v = K.cnt["v"]
        if b == 0 and grp == 0 and not getattr(self, "_dumped_qk", False):
            self._dumped_qk = True
            self.dump(getattr(self, "d_qg", None), qg[:], vwait=rope_v)
            self.dump(getattr(self, "d_kg", None), kg[:], vwait=rope_v)

        with contextlib.ExitStack() as stk:
            A = stk.enter_context
            ps_sc = A(nc.psum_tensor(K.nm("ps_sc"), [128, 2048], F32))
            ps_av = [A(nc.psum_tensor(K.nm(f"ps_av{i}"), [128, 1024], F32))
                     for i in range(2)]
            last_sc_t = 0
            for pl in range(2):
                for hl in range(2):
                    last_sc_t = self._attn_unit(causal, pl, hl, qg, kg, ao,
                                                e_sb, v_sb, ps_sc, ps_av,
                                                rope_v, ao_stage, grp)
        for pl in range(2):
            self._slot_read_t[pair_slot[pl]] = last_sc_t
        self._qg_read_t = last_sc_t
        if b == 0 and grp == 0 and not getattr(self, "_dumped_ao", False):
            self._dumped_ao = True
            self._ao_guard_dg = self.dump(getattr(self, "d_ao", None), ao[:],
                                          vwait=self.av_free_v)
            self.dump(getattr(self, "d_e", None), e_sb[:],
                      vwait=self.av_free_v)
            self.dump(getattr(self, "d_rb", None), self.rowbuf[:],
                      vwait=self.av_free_v)

        self._attn_outproj(variant, b, grp, ao, arena, xupd)

    def _proj_rope_strip(self, b, pl, qk, sx, ybuf, wqk_ap, dst, ps_qk,
                         ps_rot, wqk_dg, qsc=None):
        K = self
        cols = slice(pl * L + sx * 512, pl * L + sx * 512 + 512)
        tcols = slice(sx * 512, sx * 512 + 512)
        esb = self._qsc if qk == 0 else dst
        ecols = slice(1536, 2048) if qk == 0 else cols

        def t_f(t):
            K.wait("tensor", t, "dg", wqk_dg)
            K.wait("tensor", t, "v", self._pr_evac_v)
            for c in range(NCH):
                i = t.matmul(ps_qk[:],
                             wqk_ap[:, c * 256 + qk * 128:
                                    c * 256 + qk * 128 + 128],
                             ybuf[:, c * L + sx * 512: c * L + sx * 512 + 512],
                             start=(c == 0), stop=(c == NCH - 1))
            K.inc(i, "t", 1)
        self.tbody(t_f)
        proj_t = K.cnt["t"]

        def v_e(v):
            K.wait("vector", v, "t", proj_t)
            K.wait("vector", v, "t", self._qg_read_t)
            K.inc(v.tensor_copy(esb[:, ecols], ps_qk[:]), "v", 1)
        self.vbody(v_e)
        self._pr_evac_v = K.cnt["v"]
        evac_v = self._pr_evac_v

        def t_r(t):
            K.wait("tensor", t, "v", evac_v)
            K.wait("tensor", t, "v", self._pr_rot_used_v)
            K.inc(t.matmul(ps_rot[:], self.p32_sb[:], esb[:, ecols],
                           start=True, stop=True), "t", 1)
        self.tbody(t_r)
        rot_t = K.cnt["t"]

        def v_f(v):
            K.wait("vector", v, "t", rot_t)
            K.inc(v.tensor_tensor(K.rowbuf[:, 512:1024], ps_rot[:],
                                  self.sin_sb[:, tcols], op=ALU.mult), "v", 1)
            self._pr_rot_used_v = K.cnt["v"]
            K.inc(v.tensor_tensor(K.rowbuf[:, 1024:1536], esb[:, ecols],
                                  self.cos_sb[:, tcols], op=ALU.mult), "v", 1)
            if qk == 0:
                s0 = (2 * pl) * L + sx * 512
                s1 = (2 * pl + 1) * L + sx * 512
                K.inc(v.tensor_tensor(dst[0:64, s0:s0 + 512],
                                      K.rowbuf[0:64, 512:1024],
                                      K.rowbuf[0:64, 1024:1536],
                                      op=ALU.add), "v", 1)
                K.inc(v.tensor_tensor(dst[64:128, s1:s1 + 512],
                                      K.rowbuf[64:128, 512:1024],
                                      K.rowbuf[64:128, 1024:1536],
                                      op=ALU.add), "v", 1)
            else:
                K.inc(v.tensor_tensor(dst[:, cols], K.rowbuf[:, 512:1024],
                                      K.rowbuf[:, 1024:1536], op=ALU.add),
                      "v", 1)
        self.vbody(v_f)
        self._pr_rope_v = K.cnt["v"]

    def _attn_unit(self, causal, pl, hl, qg, kg, ao, e_sb, v_sb, ps_sc,
                   ps_av, rope_v, ao_stage, grp):
        K = self
        rows = slice(hl * 64, hl * 64 + 64)
        hp = 2 * grp + pl
        head = 2 * hp + hl

        exp_s, mask_v, av_t = {}, {}, {}
        prev_av_free = self.av_free_v
        prev_exp15 = self._prev_exp15_s
        prev_av_tail = dict(self._prev_av_tail)

        for kb in range(VKB):
            q0 = kb * 128 if causal else 0

            def t_sc(t, kb=kb, q0=q0):
                if kb == 0:
                    K.wait("tensor", t, "v", rope_v)
                    K.wait("tensor", t, "s", prev_exp15)
                else:
                    K.wait("tensor", t, "s", exp_s[kb - 1])
                lhsT = kg[:, pl * L + kb * 128: pl * L + kb * 128 + 128]
                qb = (2 * pl + hl) * L
                for j in range(q0 // 512, 4):
                    c0 = max(q0, j * 512)
                    i = t.matmul(ps_sc[:, c0:(j + 1) * 512], lhsT,
                                 qg[:, qb + c0: qb + (j + 1) * 512],
                                 start=True, stop=True)
                K.inc(i, "t", 1)
            self.tbody(t_sc)
            sc_t = K.cnt["t"]

            def s_e(s, kb=kb, q0=q0, sc_t=sc_t):
                K.wait("scalar", s, "t", sc_t)
                if kb >= 2:
                    K.wait("scalar", s, "t", av_t[kb - 2])
                elif prev_av_tail:
                    K.wait("scalar", s, "t", prev_av_tail[VKB - 2 + kb])
                K.inc(s.activation(
                    e_sb[:, (kb % 2) * L + q0: (kb % 2) * L + L],
                    ps_sc[:, q0:L], AF.Exp, scale=np.float32(0.125),
                    bias=np.float32(-80.0)), "s", 1)
            self.sbody(s_e)
            exp_s[kb] = K.cnt["s"]

            if causal:
                def v_m(v, kb=kb):
                    K.wait("vector", v, "s", exp_s[kb])
                    c0 = (kb % 2) * L + kb * 128
                    K.inc(v.tensor_tensor(e_sb[:, c0:c0 + 128],
                                          e_sb[:, c0:c0 + 128],
                                          self.diag_sb[:], op=ALU.mult),
                          "v", 1)
                self.vbody(v_m)
                mask_v[kb] = K.cnt["v"]

            def t_av(t, kb=kb, q0=q0):
                K.wait("tensor", t, "s", exp_s[kb])
                if causal:
                    K.wait("tensor", t, "v", mask_v[kb])
                if kb == 0:
                    K.wait("tensor", t, "v", prev_av_free)
                vsl = v_sb[:, kb * VROW + head * VHEAD:
                           kb * VROW + head * VHEAD + VHEAD]
                for hf in range(2):
                    h0 = hf * 1024
                    if q0 >= h0 + 1024:
                        continue
                    for j in range(2):
                        c0 = max(q0, h0 + j * 512)
                        c1 = h0 + (j + 1) * 512
                        if c0 >= c1:
                            continue
                        last_kb = (4 * (2 * hf + j) + 3) if causal \
                            else VKB - 1
                        i = t.matmul(ps_av[hf][0:65, c0 - h0:c1 - h0],
                                     vsl,
                                     e_sb[:, (kb % 2) * L + c0:
                                          (kb % 2) * L + c1],
                                     start=(kb == 0), stop=(kb == last_kb))
                K.inc(i, "t", 1)
            self.tbody(t_av)
            av_t[kb] = K.cnt["t"]

        self._prev_exp15_s = exp_s[VKB - 1]
        self._prev_av_tail = av_t
        last_av_t = av_t[VKB - 1]

        aoc = pl * L
        for hf in range(2):
            def v_d(v, hf=hf):
                K.wait("vector", v, "t", last_av_t)
                if hf == 1:
                    K.wait("vector", v, "t", self._rb_t)
                K.inc(v.reciprocal(self.rowbuf[64:65, 512:1536],
                                   ps_av[hf][64:65, :]), "v", 1)
            self.vbody(v_d)
            recip_v = K.cnt["v"]

            csrc = self.x2buf[0:64, 0:1024]

            def v_c(v, hf=hf, csrc=csrc):
                K.inc(v.tensor_copy(csrc, ps_av[hf][0:64, :]), "v", 1)
            self.vbody(v_c)
            copy_v = K.cnt["v"]

            def t_rb(t, hf=hf, copy_v=copy_v):
                K.wait("tensor", t, "v", copy_v)
                for j in range(2):
                    i = t.matmul(
                        ps_av[hf][0:64, j * 512:(j + 1) * 512],
                        self.ones_f[64:65, 0:64],
                        self.rowbuf[64:65, 512 + j * 512: 512 + (j + 1) * 512],
                        start=True, stop=True)
                K.inc(i, "t", 1)
            self.tbody(t_rb)
            self._rb_t = K.cnt["t"]
            rb_t = self._rb_t

            ndst = (ao[0:64, aoc + hf * 1024: aoc + (hf + 1) * 1024]
                    if hl == 0 else ao_stage[0:64, :])

            def v_n(v, hf=hf, rb_t=rb_t, csrc=csrc, ndst=ndst):
                K.wait("vector", v, "t", rb_t)
                K.wait("vector", v, "dg", getattr(self, "_ao_guard_dg", 0))
                if hl == 0:
                    K.wait("vector", v, "t", self._outproj_read_t)
                else:
                    K.wait("vector", v, "t", self._aost_used_t)
                if Builder.DUMP_RB:
                    K.inc(v.tensor_copy(ndst, ps_av[hf][0:64, :]), "v", 1)
                elif Builder.SKIP_NORM:
                    K.inc(v.tensor_copy(ndst, csrc), "v", 1)
                else:
                    K.inc(v.tensor_tensor(ndst, csrc, ps_av[hf][0:64, :],
                                          op=ALU.mult), "v", 1)
            self.vbody(v_n)
            norm_v = K.cnt["v"]
            self._x2_scratch_t = 0

            if hl == 1:
                def t_rl(t, hf=hf, norm_v=norm_v):
                    K.wait("tensor", t, "v", norm_v)
                    for j in range(2):
                        i = t.matmul(ps_av[hf][:, j * 512:(j + 1) * 512],
                                     self.p64_sb[:],
                                     ao_stage[:, j * 512:(j + 1) * 512],
                                     start=True, stop=True)
                    K.inc(i, "t", 1)
                self.tbody(t_rl)
                rl_t = K.cnt["t"]

                def v_r(v, hf=hf, rl_t=rl_t):
                    K.wait("vector", v, "t", rl_t)
                    K.wait("vector", v, "dg", getattr(self, "_ao_guard_dg", 0))
                    K.wait("vector", v, "t", self._outproj_read_t)
                    K.inc(v.tensor_copy(
                        ao[64:128, aoc + hf * 1024: aoc + (hf + 1) * 1024],
                        ps_av[hf][64:128, :]), "v", 1)
                self.vbody(v_r)
                self._aost_used_t = rl_t
        self.av_free_v = K.cnt["v"]
        return K.cnt["t"]

    def _attn_outproj(self, variant, b, grp, ao, arena, xupd):
        K = self
        nc = self.nc
        slot = 0
        wo_ap = arena[:, 6144:8192]
        wsrc = self.woutT if variant == "a" else self.woT
        tb = b * L

        def gw(g):
            K.wait("gpsimd", g, "dg", K.cnt["dg"])
            K.wait("gpsimd", g, "t", self._wo_used_t[slot])
            K.wait("gpsimd", g, "t", self._vproj_done_t)
            for gc in range(2):
                r0 = (2 * grp + gc) * 128
                K.inc(g.dma_start(
                    out=wo_ap[:, gc * 1024:(gc + 1) * 1024],
                    in_=wsrc[r0:r0 + 128, :]), "dg", 16)
        self.gbody(gw)
        wo_dg = self.dma_flush()

        ao_ready = self.av_free_v
        with contextlib.ExitStack() as stk:
            A = stk.enter_context
            ps_op = [A(nc.psum_tensor(K.nm(f"ps_op{i}"), [128, 1024], F32))
                     for i in range(2)]
            evac = {}
            widx = 0
            for et in range(8):
                for sx in range(4):
                    w = widx % 4
                    pw = ps_op[w // 2][:, (w % 2) * 512:(w % 2 + 1) * 512]

                    def t_f(t, et=et, sx=sx, pw=pw, widx=widx):
                        if widx == 0:
                            K.wait("tensor", t, "dg", wo_dg)
                            K.wait("tensor", t, "v", ao_ready)
                            K.wait("tensor", t, "s", self._prev_exp15_s)
                        if widx >= 4:
                            K.wait("tensor", t, "v", evac[widx - 4])
                        for gc in range(2):
                            i = t.matmul(
                                pw,
                                wo_ap[:, gc * 1024 + et * 128:
                                      gc * 1024 + et * 128 + 128],
                                ao[:, gc * L + sx * 512:
                                   gc * L + sx * 512 + 512],
                                start=(gc == 0), stop=(gc == 1))
                        K.inc(i, "t", 1)
                    self.tbody(t_f)
                    op_t = K.cnt["t"]

                    def v_f(v, et=et, sx=sx, pw=pw, widx=widx, op_t=op_t):
                        K.wait("vector", v, "t", op_t)
                        xs = self.x_sb[:, et * T + tb + sx * 512:
                                       et * T + tb + sx * 512 + 512]
                        K.inc(v.tensor_tensor(xs, xs, pw, op=ALU.add), "v", 1)
                        evac[widx] = K.cnt["v"]
                    self.vbody(v_f)
                    widx += 1
        self._wo_used_t[slot] = K.cnt["t"]
        self._outproj_read_t = K.cnt["t"]
        xupd.append(K.cnt["v"])

    def ffn_sublayer(self, final):
        K = self
        nc = self.nc
        self.sublayer_barrier()
        if self.dbg and not getattr(self, "_dumped_x", False):
            self._dumped_x = True
            self.dump(getattr(self, "d_x", None), self.x_sb[:],
                      vwait=self.x_update_v)
        with contextlib.ExitStack() as stk:
            A = stk.enter_context
            sb = nc.sbuf_tensor
            wdt = BF16 if Builder.FFN_BF16_W else FP8
            wfc8 = A(sb(K.nm("f_wfc8"), [128, NCH * HFF], FP8))
            wpj8 = A(sb(K.nm("f_wpj8"), [128, HCH * 1024], wdt))
            ybuf = A(sb(K.nm("f_y"), [128, 2 * NCH * 512], BF16))
            hbuf = A(sb(K.nm("f_h"), [128, HCH * 512], BF16))
            sqb = A(sb(K.nm("f_sq"), [128, 2 * 512], BF16))
            ost = A(sb(K.nm("f_ost"), [128, 2 * 512], F32))

            if Builder.FFN_BF16_W:
                def syw(sy):
                    K.wait("sync", sy, "dg", self.quant_dg)
                    for c in range(NCH):
                        K.inc(sy.dma_start(
                            out=wfc8[:, c * HFF:(c + 1) * HFF],
                            in_=self.wfc8h[c * 128:(c + 1) * 128, :]), "ds", 16)
                self.ybody(syw)

                def syw2(g):
                    K.wait("gpsimd", g, "dg", self.quant_dg)
                    for c4 in range(4):
                        K.inc(g.dma_start(
                            out=wpj8[:, c4 * 8 * 1024:(c4 + 1) * 8 * 1024]
                            .rearrange("p (c e) -> p c e", e=1024),
                            in_=self.wproj8h.rearrange("(c p) e -> p c e",
                                                       p=128)
                            [:, c4 * 8:(c4 + 1) * 8, :]), "dg", 16)
                self.gbody(syw2)
                w8_ds = ("ds", K.cnt["ds"])
                self._wpj_ds = ("dg", K.cnt["dg"])
            else:
                def syw(g):
                    K.wait("gpsimd", g, "dg", K.cnt["dg"])
                    K.wait("gpsimd", g, "dg", self.quant_dg)
                    for c in range(NCH):
                        K.inc(g.dma_start(
                            out=wfc8[:, c * HFF:(c + 1) * HFF],
                            in_=self.wfc8h[c * 128:(c + 1) * 128, :]), "dg", 16)
                    for c4 in range(4):
                        K.inc(g.dma_start(
                            out=wpj8[:, c4 * 8 * 1024:(c4 + 1) * 8 * 1024]
                            .rearrange("p (c e) -> p c e", e=1024),
                            in_=self.wproj8h.rearrange("(c p) e -> p c e",
                                                       p=128)
                            [:, c4 * 8:(c4 + 1) * 8, :]), "dg", 16)
                self.gbody(syw)
                fl = self.dma_flush()
                w8_ds = ("dg", fl)
                self._wpj_ds = ("dg", fl)

            ps_fc = [A(nc.psum_tensor(K.nm(f"f_fc{i}"), [128, 512], F32))
                     for i in range(2)]
            ps_pj = [A(nc.psum_tensor(K.nm(f"f_pj{i}"), [128, 512], F32))
                     for i in range(2)]
            ps_sh = A(nc.psum_tensor(K.nm("f_sh"), [128, 512], F32))
            ps_shh = A(nc.psum_tensor(K.nm("f_shh"), [128, 512], F32))
            ps_r2 = A(nc.psum_tensor(K.nm("f_r2"), [128, 512], F32))
            ps_m2 = A(nc.psum_tensor(K.nm("f_m2"), [128, 512], F32))

            gelu_s = {}
            sq_v = {}
            fc_t_of = {}
            fc_last_t = {}
            pj_evac = {}
            xupd = []
            for s_ in range(8):
                self._ffn_strip(final, s_, wfc8, wpj8, ybuf, hbuf, sqb, ost,
                                ps_fc, ps_pj, ps_sh, ps_shh, ps_r2, ps_m2,
                                w8_ds, gelu_s, sq_v, fc_t_of, fc_last_t,
                                pj_evac, xupd)
            if not final:
                self.x_update_v = self.cnt["v"]

    def _ffn_strip(self, final, s_, wfc8, wpj8, ybuf, hbuf, sqb, ost,
                   ps_fc, ps_pj, ps_sh, ps_shh, ps_r2, ps_m2, w8_ds,
                   gelu_s, sq_v, fc_t_of, fc_last_t, pj_evac, xupd):
        K = self
        t0 = s_ * 512
        ss = s_ % 2
        rb = K.rowbuf
        eps = np.float32(1e-5)

        twaits = []
        if s_ >= 2:
            twaits.append(("t", fc_last_t[s_ - 2]))
        if s_ >= 1:
            twaits.append(("v", pj_evac[s_ - 1]))
        if Builder.FFN_SERIAL:
            self.sublayer_barrier()
        if Builder.FFN_LN_PSUM_ALT:
            lnps = (ps_r2, ps_m2, ps_r2[:], ps_m2[:])
        else:
            lnps = (ps_pj[0], ps_pj[1], ps_pj[0][:], ps_pj[1][:])
        ln_v = self.ln_strip(
            t0,
            lambda c, ss=ss: ybuf[:, ss * NCH * 512 + c * 512:
                                  ss * NCH * 512 + (c + 1) * 512],
            *lnps,
            xwaits=self._x_waits(), twaits=twaits, half=True)

        if Builder.FFN_SERIAL:
            self.sublayer_barrier()
        nostats = Builder.FFN_NO_STATS or Builder.FFN_NO_LN2
        for h in range(HCH):
            def t_fc(t, h=h):
                if h == 0:
                    K.wait("tensor", t, w8_ds[0], w8_ds[1])
                    K.wait("tensor", t, "v", ln_v)
                if h >= 2:
                    K.wait("tensor", t, "s", gelu_s[(s_, h - 2)])
                elif s_ >= 1:
                    K.wait("tensor", t, "s", gelu_s[(s_ - 1, HCH - 2 + h)])
                for c in range(NCH):
                    i = t.matmul(ps_fc[h % 2][:],
                                 wfc8[:, c * HFF + h * 128:
                                      c * HFF + h * 128 + 128],
                                 ybuf[:, (s_ % 2) * NCH * 512 + c * 512:
                                      (s_ % 2) * NCH * 512 + (c + 1) * 512],
                                 start=(c == 0), stop=(c == NCH - 1))
                if h >= 1 and not nostats:
                    K.wait("tensor", t, "s", gelu_s[(s_, h - 1)])
                    K.wait("tensor", t, "v", sq_v[(s_, h - 1)])
                    t.matmul(ps_sh[0:1, :], self.ones_bf[:],
                             hbuf[:, (h - 1) * 512: h * 512],
                             start=(h == 1), stop=False)
                    t.matmul(ps_shh[0:1, :], self.ones_bf[:],
                             sqb[:, ((h - 1) % 2) * 512:
                                 ((h - 1) % 2 + 1) * 512],
                             start=(h == 1), stop=False)
                K.inc(i, "t", 1)
            self.tbody(t_fc)
            fc_t = K.cnt["t"]
            fc_t_of[(s_, h)] = fc_t
            if h == HCH - 1:
                fc_last_t[s_] = fc_t

            def s_g(s, h=h, fc_t=fc_t):
                K.wait("scalar", s, "t", fc_t)
                K.inc(s.activation(hbuf[:, h * 512:(h + 1) * 512],
                                   ps_fc[h % 2][:], AF.Gelu), "s", 1)
            self.sbody(s_g)
            gelu_s[(s_, h)] = K.cnt["s"]

            def v_q(v, h=h, fc_t=fc_t):
                K.wait("vector", v, "s", gelu_s[(s_, h)])
                if h >= 2:
                    K.wait("vector", v, "t", fc_t_of[(s_, h - 1)])
                hs = hbuf[:, h * 512:(h + 1) * 512]
                K.inc(v.tensor_tensor(sqb[:, (h % 2) * 512:(h % 2 + 1) * 512],
                                      hs, hs, op=ALU.mult), "v", 1)
            self.vbody(v_q)
            sq_v[(s_, h)] = K.cnt["v"]

        def t_st(t):
            K.wait("tensor", t, "s", gelu_s[(s_, HCH - 1)])
            K.wait("tensor", t, "v", sq_v[(s_, HCH - 1)])
            if nostats:
                K.inc(t.matmul(ps_sh[0:1, :], self.ones_bf[:],
                               hbuf[:, 0:512], start=True, stop=True), "t", 1)
                return
            t.matmul(ps_sh[0:1, :], self.ones_bf[:],
                     hbuf[:, (HCH - 1) * 512: HCH * 512],
                     start=False, stop=True)
            i = t.matmul(ps_shh[0:1, :], self.ones_bf[:],
                         sqb[:, ((HCH - 1) % 2) * 512:
                             ((HCH - 1) % 2 + 1) * 512],
                         start=False, stop=True)
            K.inc(i, "t", 1)
        self.tbody(t_st)
        st_t = K.cnt["t"]

        if s_ == 0 and self.dbg and not getattr(self, "_dumped_ffn", False):
            self._dumped_ffn = True
            self.dump(getattr(self, "d_yf", None),
                      ybuf[:, 0:NCH * 512], twait=fc_last_t[s_])
            gcount = self.dump(getattr(self, "d_h", None), hbuf[:],
                               twait=st_t)
            self._h_guard_dg = gcount

            def bar_v(v):
                K.wait("vector", v, "dg", gcount)

            def bar_s(s):
                K.wait("scalar", s, "dg", gcount)

            def bar_t(t):
                K.wait("tensor", t, "dg", gcount)
            self.vbody(bar_v)
            self.sbody(bar_s)
            self.tbody(bar_t)

        if Builder.FFN_NO_LN2:
            def t_bx(t):
                K.wait("tensor", t, "t", st_t) if False else None
                t.matmul(ps_r2[:], self.ones_f[0:1, :], rb[0:1, 0:512],
                         start=True, stop=True)
                K.inc(t.matmul(ps_m2[:], self.ones_f[0:1, :],
                               rb[0:1, 0:512], start=True, stop=True), "t", 1)
            self.tbody(t_bx)
            b2_t = K.cnt["t"]
            z_v = K.cnt["v"]
            self._ffn_proj(final, s_, wpj8, hbuf, ybuf, ost, ps_pj, z_v,
                           pj_evac, xupd, t0)
            pj_evac[s_] = K.cnt["v"]
            return

        def v_v(v):
            K.wait("vector", v, "t", st_t)
            K.inc(v.tensor_scalar_mul(rb[0:1, 0:512], ps_sh[0:1, :],
                                      1.0 / HFF), "v", 1)
            K.inc(v.tensor_tensor(rb[0:1, 512:1024], rb[0:1, 0:512],
                                  rb[0:1, 0:512], op=ALU.mult), "v", 1)
            K.inc(v.scalar_tensor_tensor(rb[0:1, 512:1024], ps_shh[0:1, :],
                                         1.0 / HFF, rb[0:1, 512:1024],
                                         op0=ALU.mult, op1=ALU.subtract),
                  "v", 1)
        self.vbody(v_v)
        var_v = K.cnt["v"]

        def s_r(s):
            K.wait("scalar", s, "v", var_v)
            K.inc(s.activation(rb[0:1, 512:1024], rb[0:1, 512:1024],
                               AF.Ln, bias=eps), "s", 1)
            K.inc(s.activation(rb[0:1, 1024:1536], rb[0:1, 512:1024],
                               AF.Exp, scale=np.float32(-0.5)), "s", 1)
        self.sbody(s_r)
        rstd_s = K.cnt["s"]

        def v_m2(v):
            K.wait("vector", v, "s", rstd_s)
            K.inc(v.tensor_tensor(rb[0:1, 0:512], rb[0:1, 0:512],
                                  rb[0:1, 1024:1536], op=ALU.mult), "v", 1)
        self.vbody(v_m2)
        mur_v = K.cnt["v"]

        def t_b2(t):
            K.wait("tensor", t, "v", mur_v)
            t.matmul(ps_r2[:], self.ones_f[0:1, :], rb[0:1, 1024:1536],
                     start=True, stop=True)
            K.inc(t.matmul(ps_m2[:], self.ones_f[0:1, :], rb[0:1, 0:512],
                           start=True, stop=True), "t", 1)
        self.tbody(t_b2)
        b2_t = K.cnt["t"]

        for h in range(HCH if not Builder.FFN_NO_LN2 else 0):
            def v_z(v, h=h):
                if h == 0:
                    K.wait("vector", v, "t", b2_t)
                    K.wait("vector", v, "t", st_t)
                    K.wait("vector", v, "dg", getattr(self, "_h_guard_dg", 0))
                hs = hbuf[:, h * 512:(h + 1) * 512]
                K.inc(v.tensor_tensor(K.rowbuf[:, 512:1024], hs, ps_r2[:],
                                      op=ALU.mult), "v", 1)
                K.inc(v.scalar_tensor_tensor(hs, ps_m2[:], -1.0,
                                             K.rowbuf[:, 512:1024],
                                             op0=ALU.mult, op1=ALU.add),
                      "v", 1)
                K.inc(v.tensor_scalar(hs, hs, self.lnwb_sb[:, h:h + 1],
                                      self.lnwb_sb[:, 32 + h:32 + h + 1],
                                      op0=ALU.mult, op1=ALU.add), "v", 1)
            self.vbody(v_z)
        z_v = K.cnt["v"]

        if Builder.FFN_SERIAL:
            self.sublayer_barrier()
        self._ffn_proj(final, s_, wpj8, hbuf, ybuf, ost, ps_pj, z_v,
                       pj_evac, xupd, t0)
        pj_evac[s_] = K.cnt["v"]

    def _ffn_proj(self, final, s_, wpj8, hbuf, ybuf, ost, ps_pj, z_v,
                  pj_evac, xupd, t0):
        K = self
        wpj_ds = getattr(self, "_wpj_ds", 0)
        if not isinstance(wpj_ds, tuple):
            wpj_ds = ("ds", wpj_ds)
        for dt_ in range(NCH):
            def t_pj(t, dt_=dt_):
                if dt_ == 0:
                    K.wait("tensor", t, "v", z_v)
                    K.wait("tensor", t, wpj_ds[0], wpj_ds[1])
                if dt_ >= 2:
                    K.wait("tensor", t, "v", pj_evac[(s_, dt_ - 2)])
                for hc in range(HCH):
                    i = t.matmul(ps_pj[dt_ % 2][:],
                                 wpj8[:, hc * 1024 + dt_ * 128:
                                      hc * 1024 + dt_ * 128 + 128],
                                 hbuf[:, hc * 512:(hc + 1) * 512],
                                 start=(hc == 0), stop=(hc == HCH - 1))
                K.inc(i, "t", 1)
            self.tbody(t_pj)
            pj_t = K.cnt["t"]

            if not final:
                def v_e(v, dt_=dt_, pj_t=pj_t):
                    K.wait("vector", v, "t", pj_t)
                    xs = self.x_sb[:, dt_ * T + t0: dt_ * T + t0 + 512]
                    K.inc(v.tensor_tensor(xs, xs, ps_pj[dt_ % 2][:],
                                          op=ALU.add), "v", 1)
                    pj_evac[(s_, dt_)] = K.cnt["v"]
                self.vbody(v_e)
            else:
                def v_e(v, dt_=dt_, pj_t=pj_t):
                    K.wait("vector", v, "t", pj_t)
                    K.wait("vector", v, "ds",
                           self._ost_ds.get((dt_ + 1) % 2, 0)
                           if hasattr(self, "_ost_ds") else 0)
                    xs = self.x_sb[:, dt_ * T + t0: dt_ * T + t0 + 512]
                    K.inc(v.tensor_tensor(ost[:, (dt_ % 2) * 512:
                                              (dt_ % 2 + 1) * 512],
                                          xs, ps_pj[dt_ % 2][:], op=ALU.add),
                          "v", 1)
                    pj_evac[(s_, dt_)] = K.cnt["v"]
                    K.inc(v.memset(self.drain_sb[:], 0.0), "v", 1)
                self.vbody(v_e)
                ev = K.cnt["v"]

                def sy_o(sy, dt_=dt_, ev=ev):
                    K.wait("sync", sy, "v", ev)
                    K.inc(sy.dma_start(
                        out=self.out[dt_ * 128:(dt_ + 1) * 128, t0:t0 + 512],
                        in_=ost[:, (dt_ % 2) * 512:(dt_ % 2 + 1) * 512]),
                        "ds", 16)
                self.ybody(sy_o)
                if not hasattr(self, "_ost_ds"):
                    self._ost_ds = {}
                self._ost_ds[dt_ % 2] = K.cnt["ds"]

    def phase_final(self, direct_out):
        K = self
        nc = self.nc
        self.sublayer_barrier()
        if not direct_out:
            with contextlib.ExitStack() as stk:
                ost = stk.enter_context(
                    nc.sbuf_tensor(K.nm("fin_ost"), [128, 2 * 512], F32))
                prev_ds = {}
                for c in range(NCH):
                    for s_ in range(8):
                        idx = c * 8 + s_

                        def v_f(v, c=c, s_=s_, idx=idx):
                            if idx == 0:
                                for sem, val in self._x_waits():
                                    K.wait("vector", v, sem, val)
                            if idx >= 2:
                                K.wait("vector", v, "ds", prev_ds[idx - 1])
                            K.inc(v.tensor_copy(
                                ost[:, (idx % 2) * 512:(idx % 2 + 1) * 512],
                                self.x_sb[:, c * T + s_ * 512:
                                          c * T + s_ * 512 + 512]), "v", 1)
                            K.inc(v.memset(self.drain_sb[:], 0.0), "v", 1)
                        self.vbody(v_f)
                        cv = K.cnt["v"]

                        def sy_f(sy, c=c, s_=s_, idx=idx, cv=cv):
                            K.wait("sync", sy, "v", cv)
                            K.inc(sy.dma_start(
                                out=self.out[c * 128:(c + 1) * 128,
                                             s_ * 512:(s_ + 1) * 512],
                                in_=ost[:, (idx % 2) * 512:(idx % 2 + 1) * 512]),
                                "ds", 16)
                        self.ybody(sy_f)
                        prev_ds[idx] = K.cnt["ds"]

        ds_final = K.cnt["ds"]
        dg_final = K.cnt["dg"]

        def g_fin(g):
            K.wait("gpsimd", g, "ds", ds_final)
            K.wait("gpsimd", g, "dg", dg_final)
        self.gbody(g_fin)


_CACHED = {}


def _build_nc(nsub):
    if nsub not in _CACHED:
        _CACHED[nsub] = Builder(nsub).build()
    return _CACHED[nsub]


def _host_constants():
    half = 32
    inv = (1.0 / (10000.0 ** (np.arange(half, dtype=np.float64) / half)))
    ang = np.arange(L, dtype=np.float64)[:, None] * inv[None, :]
    cos = np.cos(ang).T.astype(np.float32)
    sin = np.sin(ang).T.astype(np.float32)
    cosrep = np.concatenate([cos, cos, cos, cos], axis=0)
    sinrep = np.concatenate([-sin, sin, -sin, sin], axis=0)
    p32 = np.zeros((128, 128), np.float32)
    for r in range(128):
        p32[r, r ^ 32] = 1.0
    diag = (np.arange(128)[:, None] <= np.arange(128)[None, :]).astype(
        np.float32)
    ident = np.eye(128, dtype=np.float32)
    p64m = np.zeros((128, 128), np.float32)
    for m in range(64, 128):
        p64m[m - 64, m] = 1.0
    return cosrep, sinrep, p32, diag, ident, p64m


def _perm_b():
    p = np.zeros(D, np.int64)
    for h in range(16):
        base = h * 64
        p[base:base + 32] = base + 2 * np.arange(32)
        p[base + 32:base + 64] = base + 2 * np.arange(32) + 1
    return p


def kernel(L0_attn_a_x, L0_attn_a_w_qkv, L0_attn_a_w_out,
           L0_attn_b_wq, L0_attn_b_wk, L0_attn_b_wv, L0_attn_b_wo,
           L0_ffn_W_fc, L0_ffn_ln_w, L0_ffn_ln_b, L0_ffn_W_proj):
    x = np.ascontiguousarray(np.asarray(L0_attn_a_x, np.float32))
    cosrep, sinrep, p32, diag, ident, p64m = _host_constants()
    pb = _perm_b()

    wqkvT = np.ascontiguousarray(np.asarray(L0_attn_a_w_qkv, np.float32).T)
    woutT = np.ascontiguousarray(np.asarray(L0_attn_a_w_out, np.float32).T)
    wqT = np.ascontiguousarray(np.asarray(L0_attn_b_wq, np.float32).T[:, pb])
    wkT = np.ascontiguousarray(np.asarray(L0_attn_b_wk, np.float32).T[:, pb])
    wvT = np.ascontiguousarray(np.asarray(L0_attn_b_wv, np.float32).T)
    woT = np.ascontiguousarray(np.asarray(L0_attn_b_wo, np.float32).T)
    wfcT = np.ascontiguousarray(np.asarray(L0_ffn_W_fc, np.float32).T)
    wprojT = np.ascontiguousarray(np.asarray(L0_ffn_W_proj, np.float32).T)
    lnw = np.asarray(L0_ffn_ln_w, np.float32)
    lnb = np.asarray(L0_ffn_ln_b, np.float32)
    lnwb = np.zeros((128, 64), np.float32)
    lnwb[:, 0:32] = lnw.reshape(32, 128).T
    lnwb[:, 32:64] = lnb.reshape(32, 128).T

    nc = _build_nc(N_SUBLAYERS)
    shared = dict(wqkvT=wqkvT, woutT=woutT, wqT=wqT, wkT=wkT, wvT=wvT,
                  woT=woT, wfcT=wfcT, wprojT=wprojT, lnwb=lnwb,
                  cosrep=cosrep, sinrep=sinrep, p32=p32, diagm=diag,
                  ident=ident, p64m=p64m)
    in_maps = []
    for core in range(8):
        xs = x[2 * core:2 * core + 2]
        xTc = np.ascontiguousarray(
            xs.reshape(T, D).T)
        m = dict(shared)
        m["xT"] = xTc
        in_maps.append(m)

    res = run_bass_kernel_spmd(nc, in_maps, core_ids=list(range(8)))
    outs = []
    for core in range(8):
        o = res.results[core]["out"]
        outs.append(np.ascontiguousarray(o.T).reshape(BLOC, L, D))
    return np.concatenate(outs, axis=0)


# version 07
# speedup vs baseline: 1.0477x; 1.0477x over previous
import contextlib
import numpy as np
import ml_dtypes
import concourse.bass as bass
import concourse.mybir as mybir
from concourse.bass_utils import run_bass_kernel_spmd

F32 = mybir.dt.float32
BF16 = mybir.dt.bfloat16
FP8 = mybir.dt.float8e4
AF = mybir.ActivationFunctionType
ALU = mybir.AluOpType
AX = mybir.AxisListType

D = 1024
L = 2048
BLOC = 2
T = BLOC * L
NCH = D // 128
HFF = 4096
HCH = HFF // 128
N_SUBLAYERS = 4

VKB = 16
VHEAD = 65
VROW = 16 * VHEAD


class Builder:
    SKIP_NORM = False
    DUMP_RB = False
    FFN_NO_LN2 = False
    FFN_NO_STATS = False
    FFN_LN_PSUM_ALT = False
    FFN_SERIAL = False
    FFN_BF16_W = False
    QUANT_SERIAL = False
    W8_EXTOUT = False

    def __init__(self, n_sublayers=N_SUBLAYERS, dbg=False, rd=False):
        self.nc = bass.Bass("TRN2", target_bir_lowering=False,
                            detect_race_conditions=rd)
        self.nsub = n_sublayers
        self.dbg = dbg
        self.x_update_v = 0
        self.av_free_v = 0
        self._scbuf_last_exp = {0: 0, 1: 0}
        self._e_war_t = 0
        self._av_used_v = {0: 0, 1: 0}
        self._last_exp_s = 0
        self._pr_evac_v = 0
        self._pr_rope_v = 0
        self._pr_rot_used_v = 0
        self._outproj_read_t = 0
        self._wo_used_t = {0: 0, 1: 0}
        self._ybuf_read_t = 0
        self._qg_read_t = 0
        self._rb_t = 0
        self._aost_used_t = 0
        self._x2_scratch_t = 0

    def _init_sems(self, stk):
        self.sems, self.cnt, self.mark = {}, {}, {}
        for s in ("dg", "ds", "da", "t", "s", "v"):
            self.sems[s] = stk.enter_context(self.nc.semaphore(s))
            self.cnt[s] = 0

    def nm(self, base):
        self._uid = getattr(self, "_uid", 0) + 1
        return f"{base}_{self._uid}"

    def inc(self, inst, sem, n=1):
        inst.then_inc(self.sems[sem], n)
        self.cnt[sem] += n
        return self.cnt[sem]

    def wait(self, ename, eng, sem, val):
        if val <= 0:
            return
        assert val <= self.cnt[sem], (sem, val, self.cnt[sem])
        key = (ename, sem)
        if self.mark.get(key, 0) >= val:
            return
        eng.wait_ge(self.sems[sem], val)
        self.mark[key] = val

    def tbody(self, fn):
        self.block.tensor(fn)

    def vbody(self, fn):
        self.block.vector(fn)

    def sbody(self, fn):
        self.block.scalar(fn)

    def gbody(self, fn):
        self.block.gpsimd(fn)

    def ybody(self, fn):
        self.block.sync(fn)

    def build(self):
        nc = self.nc
        dt = nc.dram_tensor
        self.xT = dt("xT", [D, T], F32, kind="ExternalInput").ap()
        self.wqkvT = dt("wqkvT", [D, 3 * D], F32, kind="ExternalInput").ap()
        self.woutT = dt("woutT", [D, D], F32, kind="ExternalInput").ap()
        self.wqT = dt("wqT", [D, D], F32, kind="ExternalInput").ap()
        self.wkT = dt("wkT", [D, D], F32, kind="ExternalInput").ap()
        self.wvT = dt("wvT", [D, D], F32, kind="ExternalInput").ap()
        self.woT = dt("woT", [D, D], F32, kind="ExternalInput").ap()
        self.wfcT = dt("wfcT", [D, HFF], F32, kind="ExternalInput").ap()
        self.wprojT = dt("wprojT", [HFF, D], F32, kind="ExternalInput").ap()
        self.lnwb = dt("lnwb", [128, 64], F32, kind="ExternalInput").ap()
        self.cosrep = dt("cosrep", [128, L], F32, kind="ExternalInput").ap()
        self.sinrep = dt("sinrep", [128, L], F32, kind="ExternalInput").ap()
        self.p32 = dt("p32", [128, 128], F32, kind="ExternalInput").ap()
        self.diagm = dt("diagm", [128, 128], F32, kind="ExternalInput").ap()
        self.ident = dt("ident", [128, 128], F32, kind="ExternalInput").ap()
        self.p64m = dt("p64m", [128, 128], F32, kind="ExternalInput").ap()
        self.out = dt("out", [D, T], F32, kind="ExternalOutput").ap()
        w8kind = "ExternalOutput" if (self.dbg or Builder.W8_EXTOUT) \
            else "Internal"
        self.wfc8h = dt("wfc8h", [D, HFF], FP8, kind=w8kind).ap()
        self.wproj8h = dt("wproj8h", [HFF, D], FP8, kind=w8kind).ap()
        if self.dbg:
            self.d_y = dt("d_y", [128, NCH * L], F32, kind="ExternalOutput").ap()
            self.d_v = dt("d_v", [128, VKB * VROW], F32, kind="ExternalOutput").ap()
            self.d_qg = dt("d_qg", [128, 2 * L], F32, kind="ExternalOutput").ap()
            self.d_kg = dt("d_kg", [128, 2 * L], F32, kind="ExternalOutput").ap()
            self.d_ao = dt("d_ao", [128, 2 * L], F32, kind="ExternalOutput").ap()
            self.d_e = dt("d_e", [128, 2 * L], F32, kind="ExternalOutput").ap()
            self.d_rb = dt("d_rb", [128, 1536], F32, kind="ExternalOutput").ap()
            self.d_h = dt("d_h", [128, HCH * 512], F32, kind="ExternalOutput").ap()
            self.d_yf = dt("d_yf", [128, NCH * 512], F32, kind="ExternalOutput").ap()
            self.d_x = dt("d_x", [128, NCH * T], F32, kind="ExternalOutput").ap()
            self.d_cs = dt("d_cs", [2, 128, 80], F32, kind="ExternalOutput").ap()

        with contextlib.ExitStack() as stk:
            A = stk.enter_context
            sb = nc.sbuf_tensor
            self.x_sb = A(sb("x_sb", [128, NCH * T], BF16))
            self.cos_sb = A(sb("cos_sb", [128, L], BF16))
            self.sin_sb = A(sb("sin_sb", [128, L], BF16))
            self.p32_sb = A(sb("p32_sb", [128, 128], BF16))
            self.diag_sb = A(sb("diag_sb", [128, 128], BF16))
            self.p64_sb = A(sb("p64_sb", [128, 128], BF16))
            self.ones_bf = A(sb("ones_bf", [128, 1], BF16))
            self.ones_f = A(sb("ones_f", [128, 128], F32))
            self.rowbuf = A(sb("rowbuf", [128, 1536], F32))
            self.lnwb_sb = A(sb("lnwb_sb", [128, 64], F32))
            self.x2buf = A(sb("x2buf", [128, 4 * 512], BF16))
            self.drain_sb = A(sb("drain_sb", [128, 1], F32))
            self._init_sems(stk)
            self.block = A(nc.Block())

            self.phase_init()
            if self.nsub >= 2 or self.nsub in (-1, -2, -3):
                self.quant_phase()
            if self.nsub >= 1 or self.nsub == -1:
                self.attn_sublayer("a")
            if self.nsub >= 2 or self.nsub == -2:
                self.ffn_sublayer(final=False)
            if self.nsub >= 3:
                self.attn_sublayer("b")
            if self.nsub >= 4:
                self.ffn_sublayer(final=True)
            self.phase_final(direct_out=(self.nsub >= 4))

        return nc

    def phase_init(self):
        K = self

        def g(g):
            for c in range(NCH):
                K.inc(g.dma_start(out=K.x_sb[:, c * T:(c + 1) * T],
                                  in_=K.xT[c * 128:(c + 1) * 128, :]), "dg", 16)
            K.inc(g.dma_start(out=K.cos_sb[:], in_=K.cosrep), "dg", 16)
            K.inc(g.dma_start(out=K.sin_sb[:], in_=K.sinrep), "dg", 16)
            K.inc(g.dma_start(out=K.p32_sb[:], in_=K.p32), "dg", 16)
            K.inc(g.dma_start(out=K.diag_sb[:], in_=K.diagm), "dg", 16)
            K.inc(g.dma_start(out=K.p64_sb[:], in_=K.p64m), "dg", 16)
            K.inc(g.dma_start(out=K.lnwb_sb[:], in_=K.lnwb), "dg", 16)
        self.gbody(g)
        self.x_dg = self.dma_flush()

        def v(v):
            v.memset(K.ones_bf[:], 1.0)
            K.inc(v.memset(K.ones_f[:], 1.0), "v", 1)
            K.wait("vector", v, "dg", self.x_dg)
            K.inc(v.tensor_scalar_mul(K.lnwb_sb[:], K.lnwb_sb[:], 0.5),
                  "v", 1)
        self.vbody(v)

    def dma_flush(self):
        K = self

        def g(g):
            K.inc(g.dma_start(out=self.drain_sb[0:1, 0:1],
                              in_=self.p32[0:1, 0:1]), "dg", 16)
        self.gbody(g)
        return K.cnt["dg"]

    def dump(self, dram_ap, sbuf_ap, vwait=None, twait=None, swait=None):
        if not self.dbg:
            return 0
        K = self

        def g(g):
            if vwait:
                K.wait("gpsimd", g, "v", vwait)
            if twait:
                K.wait("gpsimd", g, "t", twait)
            if swait:
                K.wait("gpsimd", g, "s", swait)
            K.inc(g.dma_start(out=dram_ap, in_=sbuf_ap), "dg", 16)
        self.gbody(g)
        return K.cnt["dg"]

    def _x_waits(self):
        w = [("dg", self.x_dg)]
        if self.x_update_v:
            w.append(("v", self.x_update_v))
        return w

    def sublayer_barrier(self):
        K = self
        snap = dict(K.cnt)

        def mk(ename):
            def f(e):
                for sem in ("t", "v", "s", "dg", "ds"):
                    if not (ename == "tensor" and sem == "t") \
                            and not (ename == "vector" and sem == "v") \
                            and not (ename == "scalar" and sem == "s"):
                        K.wait(ename, e, sem, snap[sem])
            return f
        self.tbody(mk("tensor"))
        self.vbody(mk("vector"))
        self.sbody(mk("scalar"))
        self.gbody(mk("gpsimd"))
        self.ybody(mk("sync"))

    def ln_strip(self, t0, ydst, ps_sx, ps_sxx, ps_rb, ps_mrb,
                 xwaits=(), twaits=(), ywaits=(), half=False):
        K = self
        eps = np.float32(1e-5)
        rb = K.rowbuf
        rnd_t = {}
        for r in range(4):
            c0 = 2 * r

            def v1(v, r=r, c0=c0):
                if r == 0:
                    for sem, val in xwaits:
                        K.wait("vector", v, sem, val)
                if r >= 2:
                    K.wait("vector", v, "t", rnd_t[r - 2])
                for c in (c0, c0 + 1):
                    xs = K.x_sb[:, c * T + t0: c * T + t0 + 512]
                    K.inc(v.tensor_tensor(
                        K.x2buf[:, (c % 4) * 512:(c % 4 + 1) * 512],
                        xs, xs, op=ALU.mult), "v", 1)
            self.vbody(v1)
            x2_v = K.cnt["v"]

            def t1(t, r=r, c0=c0, x2_v=x2_v):
                if r == 0:
                    for sem, val in xwaits:
                        K.wait("tensor", t, sem, val)
                    for sem, val in twaits:
                        K.wait("tensor", t, sem, val)
                K.wait("tensor", t, "v", x2_v)
                for c in (c0, c0 + 1):
                    xs = K.x_sb[:, c * T + t0: c * T + t0 + 512]
                    t.matmul(ps_sx[0:1, :], K.ones_bf[:], xs,
                             start=(c == 0), stop=(c == NCH - 1))
                    i = t.matmul(ps_sxx[0:1, :], K.ones_bf[:],
                                 K.x2buf[:, (c % 4) * 512:(c % 4 + 1) * 512],
                                 start=(c == 0), stop=(c == NCH - 1))
                K.inc(i, "t", 1)
            self.tbody(t1)
            rnd_t[r] = K.cnt["t"]
        stats_t = K.cnt["t"]

        def v2(v):
            K.wait("vector", v, "t", stats_t)
            K.inc(v.tensor_scalar_mul(rb[0:1, 0:512], ps_sx[0:1, :],
                                      1.0 / D), "v", 1)
            K.inc(v.tensor_tensor(rb[0:1, 512:1024], rb[0:1, 0:512],
                                  rb[0:1, 0:512], op=ALU.mult), "v", 1)
            K.inc(v.scalar_tensor_tensor(rb[0:1, 512:1024], ps_sxx[0:1, :],
                                         1.0 / D, rb[0:1, 512:1024],
                                         op0=ALU.mult, op1=ALU.subtract), "v", 1)
        self.vbody(v2)
        var_v = K.cnt["v"]

        def s2(s):
            K.wait("scalar", s, "v", var_v)
            K.inc(s.activation(rb[0:1, 512:1024], rb[0:1, 512:1024],
                               AF.Ln, bias=eps), "s", 1)
            K.inc(s.activation(rb[0:1, 1024:1536], rb[0:1, 512:1024],
                               AF.Exp, scale=np.float32(-0.5),
                               bias=np.float32(-0.6931471805599453) if half
                               else np.float32(0.0)), "s", 1)
        self.sbody(s2)
        rstd_s = K.cnt["s"]

        def v3(v):
            K.wait("vector", v, "s", rstd_s)
            K.inc(v.tensor_tensor(rb[0:1, 0:512], rb[0:1, 0:512],
                                  rb[0:1, 1024:1536], op=ALU.mult), "v", 1)
        self.vbody(v3)
        mur_v = K.cnt["v"]

        def t2(t):
            K.wait("tensor", t, "v", mur_v)
            t.matmul(ps_rb, K.ones_f[0:1, :], rb[0:1, 1024:1536],
                     start=True, stop=True)
            i = t.matmul(ps_mrb, K.ones_f[0:1, :], rb[0:1, 0:512],
                         start=True, stop=True)
            K.inc(i, "t", 1)
        self.tbody(t2)
        bc_t = K.cnt["t"]

        def v4(v):
            K.wait("vector", v, "t", bc_t)
            for sem, val in twaits:
                K.wait("vector", v, sem, val)
            for sem, val in ywaits:
                K.wait("vector", v, sem, val)
            for c in range(NCH):
                xs = K.x_sb[:, c * T + t0: c * T + t0 + 512]
                K.inc(v.tensor_tensor(K.rowbuf[:, 512:1024], xs, ps_rb,
                                      op=ALU.mult), "v", 1)
                K.inc(v.scalar_tensor_tensor(ydst(c), ps_mrb, -1.0,
                                             K.rowbuf[:, 512:1024],
                                             op0=ALU.mult,
                                             op1=ALU.add), "v", 1)
        self.vbody(v4)
        return K.cnt["v"]

    def quant_phase(self):
        K = self
        nc = self.nc
        with contextlib.ExitStack() as stk:
            A = stk.enter_context
            wst = A(nc.sbuf_tensor(K.nm("q_wst"), [128, 4 * 512], F32))
            qst = A(nc.sbuf_tensor(K.nm("q_qst"), [128, 4 * 512], BF16))
            sgn = A(nc.sbuf_tensor(K.nm("q_sgn"), [128, 4 * 512], BF16))
            colst = A(nc.sbuf_tensor(K.nm("q_colst"), [128, 80], F32))
            ps_g = A(nc.psum_tensor(K.nm("q_psg"), [128, 512], F32))

            for wi in range(2):
                wsrc = self.wfcT if wi == 0 else self.wprojT
                w8dst = self.wfc8h if wi == 0 else self.wproj8h
                nrow_ch = (D if wi == 0 else HFF) // 128
                ncol_ch = (HFF if wi == 0 else D) // 512
                chunks = [(r, cc) for r in range(nrow_ch)
                          for cc in range(ncol_ch)]
                nchunks = len(chunks)
                assert nchunks == 64
                ngrp = nchunks // 2

                s_prev = K.cnt["s"]
                red_v = {}
                for g in range(ngrp):
                    def gw1(gq, g=g):
                        K.wait("gpsimd", gq, "dg", K.cnt["dg"])
                        if g >= 2:
                            K.wait("gpsimd", gq, "v", red_v[g - 2])
                        else:
                            K.wait("gpsimd", gq, "s", s_prev)
                        for c in (2 * g, 2 * g + 1):
                            r, cc = chunks[c]
                            K.inc(gq.dma_start(
                                out=wst[:, (c % 4) * 512:(c % 4 + 1) * 512],
                                in_=wsrc[r * 128:(r + 1) * 128,
                                         cc * 512:(cc + 1) * 512]), "dg", 16)
                    self.gbody(gw1)
                    gw_dg = K.cnt["dg"]

                    def vr(v, g=g, gw_dg=gw_dg):
                        K.wait("vector", v, "dg", gw_dg)
                        for c in (2 * g, 2 * g + 1):
                            K.inc(v.reduce_sum(
                                colst[:, c:c + 1],
                                wst[:, (c % 4) * 512:(c % 4 + 1) * 512],
                                axis=AX.X, apply_absolute_value=True), "v", 1)
                    self.vbody(vr)
                    red_v[g] = K.cnt["v"]
                p1_v = K.cnt["v"]

                def v_tot(v):
                    K.inc(v.reduce_sum(colst[:, 77:78], colst[:, 0:nchunks],
                                       axis=AX.X), "v", 1)
                self.vbody(v_tot)
                tot_v = K.cnt["v"]

                def t_g(t):
                    K.wait("tensor", t, "v", tot_v)
                    K.inc(t.matmul(ps_g[0:1, 0:1], colst[:, 77:78],
                                   self.ones_f[:, 0:1], start=True, stop=True),
                          "t", 1)
                self.tbody(t_g)
                g_t = K.cnt["t"]
                nel = float(D * HFF)

                def v_g(v):
                    K.wait("vector", v, "t", g_t)
                    K.inc(v.tensor_scalar(K.rowbuf[0:1, 0:1], ps_g[0:1, 0:1],
                                          0.5 / nel, 0.5e-6,
                                          op0=ALU.mult, op1=ALU.add), "v", 1)
                    K.inc(v.tensor_scalar(K.rowbuf[0:1, 1:2], ps_g[0:1, 0:1],
                                          -0.5 / nel, -0.5e-6,
                                          op0=ALU.mult, op1=ALU.add), "v", 1)
                self.vbody(v_g)
                th_s = K.cnt["v"]

                def t_b(t):
                    K.wait("tensor", t, "v", th_s)
                    t.matmul(ps_g[:, 0:1], self.ones_f[0:1, :],
                             K.rowbuf[0:1, 0:1], start=True, stop=True)
                    K.inc(t.matmul(ps_g[:, 1:2], self.ones_f[0:1, :],
                                   K.rowbuf[0:1, 1:2], start=True, stop=True),
                          "t", 1)
                self.tbody(t_b)
                thb_t = K.cnt["t"]
                s_prev2 = K.cnt["s"]

                def v_th(v):
                    K.wait("vector", v, "t", thb_t)
                    K.wait("vector", v, "s", s_prev2)
                    K.inc(v.tensor_copy(colst[:, 78:80], ps_g[:, 0:2]), "v", 1)
                self.vbody(v_th)
                th_v = K.cnt["v"]

                if self.dbg:
                    self.dump(getattr(self, "d_cs", None)[wi], colst[:],
                              vwait=K.cnt["v"])

                dg2s = K.cnt["dg"]
                sgn_s = {}
                qv = {}
                sgrp_w = {}
                for g in range(ngrp):
                    def gw2(gq, g=g):
                        K.wait("gpsimd", gq, "dg", K.cnt["dg"])
                        if g >= 2:
                            K.wait("gpsimd", gq, "s", sgn_s[g - 2])
                        else:
                            K.wait("gpsimd", gq, "v", p1_v)
                        for c in (2 * g, 2 * g + 1):
                            r, cc = chunks[c]
                            K.inc(gq.dma_start(
                                out=wst[:, (c % 4) * 512:(c % 4 + 1) * 512],
                                in_=wsrc[r * 128:(r + 1) * 128,
                                         cc * 512:(cc + 1) * 512]), "dg", 16)
                    self.gbody(gw2)
                    l_dg = K.cnt["dg"]

                    def s2(s, g=g, l_dg=l_dg):
                        K.wait("scalar", s, "dg", l_dg)
                        K.wait("scalar", s, "v", th_v)
                        if g >= 1:
                            K.wait("scalar", s, "v", qv[g - 1])
                        for c in (2 * g, 2 * g + 1):
                            ws = wst[:, (c % 4) * 512:(c % 4 + 1) * 512]
                            K.inc(s.sign(sgn[:, (c % 4) * 512:
                                             (c % 4 + 1) * 512],
                                         ws, bias=colst[:, 78:79]), "s", 1)
                            K.inc(s.sign(K.rowbuf[:, (c % 2) * 512:
                                                  (c % 2) * 512 + 512],
                                         ws, bias=colst[:, 79:80]), "s", 1)
                    self.sbody(s2)
                    sgn_s[g] = K.cnt["s"]

                    def v2(v, g=g):
                        K.wait("vector", v, "s", sgn_s[g])
                        if g >= 2:
                            K.wait("vector", v, "dg", sgrp_w[g - 2])
                        else:
                            K.wait("vector", v, "dg", dg2s)
                        for c in (2 * g, 2 * g + 1):
                            K.inc(v.tensor_tensor(
                                qst[:, (c % 4) * 512:(c % 4 + 1) * 512],
                                sgn[:, (c % 4) * 512:(c % 4 + 1) * 512],
                                K.rowbuf[:, (c % 2) * 512:(c % 2) * 512 + 512],
                                op=ALU.add), "v", 1)
                    self.vbody(v2)
                    qv[g] = K.cnt["v"]

                    def gw3(gq, g=g):
                        K.wait("gpsimd", gq, "dg", K.cnt["dg"])
                        K.wait("gpsimd", gq, "v", qv[g])
                        for c in (2 * g, 2 * g + 1):
                            r, cc = chunks[c]
                            K.inc(gq.dma_start(
                                out=w8dst[r * 128:(r + 1) * 128,
                                          cc * 512:(cc + 1) * 512],
                                in_=qst[:, (c % 4) * 512:(c % 4 + 1) * 512]),
                                "dg", 16)
                    self.gbody(gw3)
                    sgrp_w[g] = K.cnt["dg"]
        self.quant_dg = K.cnt["dg"]

    def attn_sublayer(self, variant):
        K = self
        nc = self.nc
        causal = (variant == "b")
        self.sublayer_barrier()
        with contextlib.ExitStack() as stk:
            A = stk.enter_context
            sb = nc.sbuf_tensor
            ybuf = A(sb(K.nm("a_y"), [128, NCH * L], BF16))
            v_sb = A(sb(K.nm("a_v"), [128, VKB * VROW], BF16))
            ao_stage = A(sb(K.nm("a_aost"), [128, 1024], BF16))
            qg = A(sb(K.nm("a_qg"), [128, 4 * L], BF16))
            kg = A(sb(K.nm("a_kg"), [128, 2 * L], BF16))
            qsc = self.x2buf
            ao = A(sb(K.nm("a_ao"), [128, 2 * L], BF16))
            e_sb = A(sb(K.nm("a_e"), [128, 2 * L], BF16))
            arena = A(sb(K.nm("a_arena"), [128, 8192], BF16))

            def vz(v):
                ap = v_sb[:].rearrange("p (k c) -> p k c", c=VHEAD)
                v.memset(ap[:, :, 64:65], 1.0)
                v.memset(ao_stage[64:128, :], 0.0)
                for pl in range(2):
                    v.memset(qg[64:128, (2 * pl) * L:(2 * pl + 1) * L], 0.0)
                    v.memset(qg[0:64, (2 * pl + 1) * L:(2 * pl + 2) * L], 0.0)
                K.inc(v.memset(self.drain_sb[:], 0.0), "v", 1)
            self.vbody(vz)
            self._vz_v = K.cnt["v"]

            self._slot_read_t = {0: 0, 1: 0, 2: 0}
            self._vproj_done_t = 0
            self._qsc = qsc
            xupd = []
            for b in range(BLOC):
                self._attn_batch(variant, causal, b, ybuf, v_sb, qg, kg, ao,
                                 e_sb, arena, ao_stage, xupd)
            self.x_update_v = max(xupd)

    def _attn_batch(self, variant, causal, b, ybuf, v_sb, qg, kg, ao, e_sb,
                    arena, ao_stage, xupd):
        K = self
        nc = self.nc
        tb = b * L

        with contextlib.ExitStack() as stk:
            A = stk.enter_context
            ps_sx = A(nc.psum_tensor(K.nm("ps_sx"), [128, 512], F32))
            ps_sxx = A(nc.psum_tensor(K.nm("ps_sxx"), [128, 512], F32))
            ps_rb = A(nc.psum_tensor(K.nm("ps_rb"), [128, 512], F32))
            ps_mrb = A(nc.psum_tensor(K.nm("ps_mrb"), [128, 512], F32))
            ps_v = [A(nc.psum_tensor(K.nm(f"ps_v{i}"), [128, 1024], F32))
                    for i in range(2)]

            ywaits = []
            if b == 1:
                ywaits.append(("t", self._ybuf_read_t))
            for sx in range(4):
                self.ln_strip(
                    tb + sx * 512,
                    lambda c, sx=sx: ybuf[:, c * L + sx * 512:
                                          c * L + sx * 512 + 512],
                    ps_sx, ps_sxx, ps_rb[:], ps_mrb[:],
                    xwaits=self._x_waits(), ywaits=ywaits)
            y_v = K.cnt["v"]
            self._y_ready_v = y_v
            if b == 0 and not getattr(self, "_dumped_y", False):
                self._dumped_y = True
                self.dump(getattr(self, "d_y", None), ybuf[:], vwait=y_v)

            def gw(g):
                K.wait("gpsimd", g, "dg", K.cnt["dg"])
                K.wait("gpsimd", g, "t", K.cnt["t"])
                wsrc = (self.wqkvT[:, 2 * D:3 * D] if variant == "a"
                        else self.wvT)
                for c in range(NCH):
                    K.inc(g.dma_start(out=arena[:, c * 1024:(c + 1) * 1024],
                                      in_=wsrc[c * 128:(c + 1) * 128, :]),
                          "dg", 16)
            self.gbody(gw)
            wv_dg = self.dma_flush()

            evac_v = {}
            for tt in range(16):
                def t_f(t, tt=tt):
                    if tt == 0:
                        K.wait("tensor", t, "dg", wv_dg)
                        K.wait("tensor", t, "v", max(y_v, self._vz_v))
                    if tt >= 2:
                        K.wait("tensor", t, "v", evac_v[tt - 2])
                    for c in range(NCH):
                        ysl = ybuf[:, c * L + tt * 128: c * L + tt * 128 + 128]
                        for eh in range(2):
                            i = t.matmul(
                                ps_v[tt % 2][:, eh * 512:(eh + 1) * 512], ysl,
                                arena[:, c * 1024 + eh * 512:
                                      c * 1024 + (eh + 1) * 512],
                                start=(c == 0), stop=(c == NCH - 1))
                    K.inc(i, "t", 1)
                self.tbody(t_f)
                vp_t = K.cnt["t"]

                def v_f(v, tt=tt, vp_t=vp_t):
                    K.wait("vector", v, "t", vp_t)
                    for eh in range(2):
                        src = ps_v[tt % 2][:, eh * 512:(eh + 1) * 512]\
                            .rearrange("p (h e) -> p h e", e=64)
                        base = tt * VROW + 8 * eh * VHEAD
                        dst = v_sb[:, base: base + 8 * VHEAD].rearrange(
                            "p (h c) -> p h c", c=VHEAD)
                        K.inc(v.tensor_copy(dst[:, :, 0:64], src), "v", 1)
                    evac_v[tt] = K.cnt["v"]
                self.vbody(v_f)

        self._vproj_done_t = self.cnt["t"]
        if b == 0 and not getattr(self, "_dumped_v", False):
            self._dumped_v = True
            self.dump(getattr(self, "d_v", None), v_sb[:],
                      vwait=self.cnt["v"])
        for grp in range(4):
            self._attn_group(variant, causal, b, grp, ybuf, v_sb, qg, kg, ao,
                             e_sb, arena, ao_stage, xupd)
        self._ybuf_read_t = self.cnt["t"]

    def _attn_group(self, variant, causal, b, grp, ybuf, v_sb, qg, kg, ao,
                    e_sb, arena, ao_stage, xupd):
        K = self
        nc = self.nc
        slot_base = {0: 4096, 1: 0, 2: 2048}
        pair_slot = {}

        for pl in range(2):
            hp = 2 * grp + pl
            p_glob = hp
            slot = p_glob % 3
            pair_slot[pl] = slot

            def gw(g, hp=hp, slot=slot):
                K.wait("gpsimd", g, "dg", K.cnt["dg"])
                K.wait("gpsimd", g, "t", self._slot_read_t[slot])
                K.wait("gpsimd", g, "t", self._vproj_done_t)
                if variant == "a":
                    qsrc = self.wqkvT[:, hp * 128:(hp + 1) * 128]
                    ksrc = self.wqkvT[:, D + hp * 128: D + (hp + 1) * 128]
                else:
                    qsrc = self.wqT[:, hp * 128:(hp + 1) * 128]
                    ksrc = self.wkT[:, hp * 128:(hp + 1) * 128]
                dst = arena[:, slot_base[slot]:slot_base[slot] + 2048]\
                    .rearrange("p (c two e) -> p c two e", two=2, e=128)
                K.inc(g.dma_start(out=dst[:, :, 0, :],
                                  in_=qsrc.rearrange("(c p) e -> p c e",
                                                     p=128)), "dg", 16)
                K.inc(g.dma_start(out=dst[:, :, 1, :],
                                  in_=ksrc.rearrange("(c p) e -> p c e",
                                                     p=128)), "dg", 16)
            self.gbody(gw)
        wqk_dg = self.dma_flush()

        with contextlib.ExitStack() as stk:
            A = stk.enter_context
            ps_qk = A(nc.psum_tensor(K.nm("ps_qk"), [128, 512], F32))
            ps_rot = A(nc.psum_tensor(K.nm("ps_rot"), [128, 512], F32))
            for pl in range(2):
                wqk_ap = arena[:, slot_base[pair_slot[pl]]:
                               slot_base[pair_slot[pl]] + 2048]
                for qk, dst in ((0, qg), (1, kg)):
                    for sx in range(4):
                        self._proj_rope_strip(b, pl, qk, sx, ybuf, wqk_ap,
                                              dst, ps_qk, ps_rot, wqk_dg)
        rope_v = K.cnt["v"]
        if b == 0 and grp == 0 and not getattr(self, "_dumped_qk", False):
            self._dumped_qk = True
            self.dump(getattr(self, "d_qg", None), qg[:], vwait=rope_v)
            self.dump(getattr(self, "d_kg", None), kg[:], vwait=rope_v)

        with contextlib.ExitStack() as stk:
            A = stk.enter_context
            ps_sc = [A(nc.psum_tensor(K.nm(f"ps_sc{i}"), [128, 1024], F32))
                     for i in range(2)]
            ps_av = [A(nc.psum_tensor(K.nm(f"ps_av{i}"), [128, 1024], F32))
                     for i in range(2)]
            last_sc_t = 0
            for pl in range(2):
                for hl in range(2):
                    last_sc_t = self._attn_unit(causal, pl, hl, qg, kg, ao,
                                                e_sb, v_sb, ps_sc, ps_av,
                                                rope_v, ao_stage, grp)
        for pl in range(2):
            self._slot_read_t[pair_slot[pl]] = last_sc_t
        self._qg_read_t = last_sc_t
        if b == 0 and grp == 0 and not getattr(self, "_dumped_ao", False):
            self._dumped_ao = True
            self._ao_guard_dg = self.dump(getattr(self, "d_ao", None), ao[:],
                                          vwait=self.av_free_v)
            self.dump(getattr(self, "d_e", None), e_sb[:],
                      vwait=self.av_free_v)
            self.dump(getattr(self, "d_rb", None), self.rowbuf[:],
                      vwait=self.av_free_v)

        self._attn_outproj(variant, b, grp, ao, arena, xupd)

    def _proj_rope_strip(self, b, pl, qk, sx, ybuf, wqk_ap, dst, ps_qk,
                         ps_rot, wqk_dg, qsc=None):
        K = self
        cols = slice(pl * L + sx * 512, pl * L + sx * 512 + 512)
        tcols = slice(sx * 512, sx * 512 + 512)
        esb = self._qsc if qk == 0 else dst
        ecols = slice(1536, 2048) if qk == 0 else cols

        def t_f(t):
            K.wait("tensor", t, "dg", wqk_dg)
            K.wait("tensor", t, "v", self._pr_evac_v)
            for c in range(NCH):
                i = t.matmul(ps_qk[:],
                             wqk_ap[:, c * 256 + qk * 128:
                                    c * 256 + qk * 128 + 128],
                             ybuf[:, c * L + sx * 512: c * L + sx * 512 + 512],
                             start=(c == 0), stop=(c == NCH - 1))
            K.inc(i, "t", 1)
        self.tbody(t_f)
        proj_t = K.cnt["t"]

        def v_e(v):
            K.wait("vector", v, "t", proj_t)
            K.wait("vector", v, "t", self._qg_read_t)
            K.inc(v.tensor_copy(esb[:, ecols], ps_qk[:]), "v", 1)
        self.vbody(v_e)
        self._pr_evac_v = K.cnt["v"]
        evac_v = self._pr_evac_v

        def t_r(t):
            K.wait("tensor", t, "v", evac_v)
            K.wait("tensor", t, "v", self._pr_rot_used_v)
            K.inc(t.matmul(ps_rot[:], self.p32_sb[:], esb[:, ecols],
                           start=True, stop=True), "t", 1)
        self.tbody(t_r)
        rot_t = K.cnt["t"]

        def v_f(v):
            K.wait("vector", v, "t", rot_t)
            K.inc(v.tensor_tensor(K.rowbuf[:, 512:1024], ps_rot[:],
                                  self.sin_sb[:, tcols], op=ALU.mult), "v", 1)
            self._pr_rot_used_v = K.cnt["v"]
            K.inc(v.tensor_tensor(K.rowbuf[:, 1024:1536], esb[:, ecols],
                                  self.cos_sb[:, tcols], op=ALU.mult), "v", 1)
            if qk == 0:
                s0 = (2 * pl) * L + sx * 512
                s1 = (2 * pl + 1) * L + sx * 512
                K.inc(v.tensor_tensor(dst[0:64, s0:s0 + 512],
                                      K.rowbuf[0:64, 512:1024],
                                      K.rowbuf[0:64, 1024:1536],
                                      op=ALU.add), "v", 1)
                K.inc(v.tensor_tensor(dst[64:128, s1:s1 + 512],
                                      K.rowbuf[64:128, 512:1024],
                                      K.rowbuf[64:128, 1024:1536],
                                      op=ALU.add), "v", 1)
            else:
                K.inc(v.tensor_tensor(dst[:, cols], K.rowbuf[:, 512:1024],
                                      K.rowbuf[:, 1024:1536], op=ALU.add),
                      "v", 1)
        self.vbody(v_f)
        self._pr_rope_v = K.cnt["v"]

    def _attn_unit(self, causal, pl, hl, qg, kg, ao, e_sb, v_sb, ps_sc,
                   ps_av, rope_v, ao_stage, grp):
        K = self
        hp = 2 * grp + pl
        head = 2 * hp + hl
        aoc = pl * L
        qb = (2 * pl + hl) * L

        per_qh = {}
        for qh in range(2):
            qlo = qh * 1024
            kbs = [kb for kb in range(VKB)
                   if (not causal) or kb * 128 < qlo + 1024]
            assert len(kbs) % 2 == 0
            sc_t, exp_b, mask_v, av_t = {}, {}, {}, {}
            scbuf_prev = dict(self._scbuf_last_exp)
            e_war = self._e_war_t

            for idx, kb in enumerate(kbs):
                q0 = kb * 128 if causal else 0
                r0 = max(q0 - qlo, 0)

                def t_sc(t, kb=kb, qh=qh, qlo=qlo, idx=idx, r0=r0,
                         kbs=kbs, scbuf_prev=scbuf_prev):
                    if qh == 0 and kb == 0:
                        K.wait("tensor", t, "v", rope_v)
                    if idx >= 2:
                        K.wait("tensor", t, "s", exp_b[kbs[idx - 2]])
                    else:
                        K.wait("tensor", t, "s", scbuf_prev[idx % 2])
                    lhsT = kg[:, pl * L + kb * 128: pl * L + kb * 128 + 128]
                    for j in range(2):
                        c0 = max(qlo + r0, qlo + j * 512)
                        c1 = qlo + (j + 1) * 512
                        if c0 >= c1:
                            continue
                        i = t.matmul(ps_sc[idx % 2][:, c0 - qlo:c1 - qlo],
                                     lhsT, qg[:, qb + c0: qb + c1],
                                     start=True, stop=True)
                    K.inc(i, "t", 1)
                self.tbody(t_sc)
                sc_t[kb] = K.cnt["t"]

                if idx % 2 == 1:
                    def s_e(s, qh=qh, qlo=qlo, idx=idx, kbs=kbs,
                            e_war=e_war):
                        K.wait("scalar", s, "t", sc_t[kbs[idx]])
                        if idx >= 3:
                            K.wait("scalar", s, "t", av_t[kbs[idx - 2]])
                        else:
                            K.wait("scalar", s, "t", e_war)
                        for k2 in (kbs[idx - 1], kbs[idx]):
                            rr = max((k2 * 128 if causal else 0) - qlo, 0)
                            K.inc(s.activation(
                                e_sb[:, (k2 % 2) * 1024 + rr:
                                     (k2 % 2) * 1024 + 1024],
                                ps_sc[k2 % 2][:, rr:1024], AF.Exp,
                                scale=np.float32(0.125),
                                bias=np.float32(-80.0)), "s", 1)
                    self.sbody(s_e)
                    exp_b[kbs[idx - 1]] = exp_b[kb] = K.cnt["s"]

                    if causal:
                        def v_m(v, qh=qh, qlo=qlo, idx=idx, kbs=kbs):
                            K.wait("vector", v, "s", exp_b[kbs[idx]])
                            for k2 in (kbs[idx - 1], kbs[idx]):
                                if not (qlo <= k2 * 128 < qlo + 1024):
                                    continue
                                c0 = (k2 % 2) * 1024 + k2 * 128 - qlo
                                K.inc(v.tensor_tensor(
                                    e_sb[:, c0:c0 + 128],
                                    e_sb[:, c0:c0 + 128],
                                    self.diag_sb[:], op=ALU.mult), "v", 1)
                        self.vbody(v_m)
                        mask_v[kbs[idx - 1]] = mask_v[kb] = K.cnt["v"]

                    for k2 in (kbs[idx - 1], kb):
                        def t_av(t, kb=k2, qh=qh, qlo=qlo):
                            q0 = kb * 128 if causal else 0
                            K.wait("tensor", t, "s", exp_b[kb])
                            if causal:
                                K.wait("tensor", t, "v", mask_v[kb])
                            if kb == kbs[0]:
                                K.wait("tensor", t, "v",
                                       self._av_used_v[qh])
                            vsl = v_sb[:, kb * VROW + head * VHEAD:
                                       kb * VROW + head * VHEAD + VHEAD]
                            for j in range(2):
                                c0 = max(q0, qlo + j * 512)
                                c1 = qlo + (j + 1) * 512
                                if c0 >= c1:
                                    continue
                                w = qh * 2 + j
                                last_kb = (4 * w + 3) if causal else VKB - 1
                                i = t.matmul(
                                    ps_av[qh][0:65, c0 - qlo:c1 - qlo], vsl,
                                    e_sb[:, (kb % 2) * 1024 + c0 - qlo:
                                         (kb % 2) * 1024 + c1 - qlo],
                                    start=(kb == 0), stop=(kb == last_kb))
                            K.inc(i, "t", 1)
                        self.tbody(t_av)
                        av_t[k2] = K.cnt["t"]

            self._scbuf_last_exp[0] = exp_b[kbs[-2]]
            self._scbuf_last_exp[1] = exp_b[kbs[-1]]
            self._e_war_t = av_t[kbs[-1]]
            self._last_exp_s = exp_b[kbs[-1]]
            per_qh[qh] = av_t[kbs[-1]]

        for qh in range(2):
            last_av_t = per_qh[qh]

            def v_d(v, qh=qh, last_av_t=last_av_t):
                K.wait("vector", v, "t", last_av_t)
                K.wait("vector", v, "t", self._rb_t)
                K.inc(v.reciprocal(self.rowbuf[64:65, 512:1536],
                                   ps_av[qh][64:65, :]), "v", 1)
            self.vbody(v_d)

            csrc = self.x2buf[0:64, 0:1024]

            def v_c(v, qh=qh, csrc=csrc):
                K.inc(v.tensor_copy(csrc, ps_av[qh][0:64, :]), "v", 1)
            self.vbody(v_c)
            copy_v = K.cnt["v"]

            def t_rb(t, qh=qh, copy_v=copy_v):
                K.wait("tensor", t, "v", copy_v)
                for j in range(2):
                    i = t.matmul(
                        ps_av[qh][0:64, j * 512:(j + 1) * 512],
                        self.ones_f[64:65, 0:64],
                        self.rowbuf[64:65, 512 + j * 512: 512 + (j + 1) * 512],
                        start=True, stop=True)
                K.inc(i, "t", 1)
            self.tbody(t_rb)
            self._rb_t = K.cnt["t"]
            rb_t = self._rb_t

            ndst = (ao[0:64, aoc + qh * 1024: aoc + (qh + 1) * 1024]
                    if hl == 0 else ao_stage[0:64, :])

            def v_n(v, qh=qh, rb_t=rb_t, csrc=csrc, ndst=ndst):
                K.wait("vector", v, "t", rb_t)
                K.wait("vector", v, "dg", getattr(self, "_ao_guard_dg", 0))
                if hl == 0:
                    K.wait("vector", v, "t", self._outproj_read_t)
                else:
                    K.wait("vector", v, "t", self._aost_used_t)
                K.inc(v.tensor_tensor(ndst, csrc, ps_av[qh][0:64, :],
                                      op=ALU.mult), "v", 1)
            self.vbody(v_n)
            norm_v = K.cnt["v"]

            if hl == 0:
                self._av_used_v[qh] = K.cnt["v"]
            else:
                def t_rl(t, qh=qh, norm_v=norm_v):
                    K.wait("tensor", t, "v", norm_v)
                    for j in range(2):
                        i = t.matmul(ps_av[qh][:, j * 512:(j + 1) * 512],
                                     self.p64_sb[:],
                                     ao_stage[:, j * 512:(j + 1) * 512],
                                     start=True, stop=True)
                    K.inc(i, "t", 1)
                self.tbody(t_rl)
                rl_t = K.cnt["t"]

                def v_r(v, qh=qh, rl_t=rl_t):
                    K.wait("vector", v, "t", rl_t)
                    K.wait("vector", v, "dg", getattr(self, "_ao_guard_dg", 0))
                    K.wait("vector", v, "t", self._outproj_read_t)
                    K.inc(v.tensor_copy(
                        ao[64:128, aoc + qh * 1024: aoc + (qh + 1) * 1024],
                        ps_av[qh][64:128, :]), "v", 1)
                self.vbody(v_r)
                self._aost_used_t = rl_t
                self._av_used_v[qh] = K.cnt["v"]
        self.av_free_v = K.cnt["v"]
        return K.cnt["t"]

    def _attn_outproj(self, variant, b, grp, ao, arena, xupd):
        K = self
        nc = self.nc
        slot = 0
        wo_ap = arena[:, 6144:8192]
        wsrc = self.woutT if variant == "a" else self.woT
        tb = b * L

        def gw(g):
            K.wait("gpsimd", g, "dg", K.cnt["dg"])
            K.wait("gpsimd", g, "t", self._wo_used_t[slot])
            K.wait("gpsimd", g, "t", self._vproj_done_t)
            for gc in range(2):
                r0 = (2 * grp + gc) * 128
                K.inc(g.dma_start(
                    out=wo_ap[:, gc * 1024:(gc + 1) * 1024],
                    in_=wsrc[r0:r0 + 128, :]), "dg", 16)
        self.gbody(gw)
        wo_dg = self.dma_flush()

        ao_ready = self.av_free_v
        with contextlib.ExitStack() as stk:
            A = stk.enter_context
            ps_op = [A(nc.psum_tensor(K.nm(f"ps_op{i}"), [128, 1024], F32))
                     for i in range(2)]
            evac = {}
            widx = 0
            for et in range(8):
                for sx in range(4):
                    w = widx % 4
                    pw = ps_op[w // 2][:, (w % 2) * 512:(w % 2 + 1) * 512]

                    def t_f(t, et=et, sx=sx, pw=pw, widx=widx):
                        if widx == 0:
                            K.wait("tensor", t, "dg", wo_dg)
                            K.wait("tensor", t, "v", ao_ready)
                            K.wait("tensor", t, "s", self._last_exp_s)
                        if widx >= 4:
                            K.wait("tensor", t, "v", evac[widx - 4])
                        for gc in range(2):
                            i = t.matmul(
                                pw,
                                wo_ap[:, gc * 1024 + et * 128:
                                      gc * 1024 + et * 128 + 128],
                                ao[:, gc * L + sx * 512:
                                   gc * L + sx * 512 + 512],
                                start=(gc == 0), stop=(gc == 1))
                        K.inc(i, "t", 1)
                    self.tbody(t_f)
                    op_t = K.cnt["t"]

                    def v_f(v, et=et, sx=sx, pw=pw, widx=widx, op_t=op_t):
                        K.wait("vector", v, "t", op_t)
                        xs = self.x_sb[:, et * T + tb + sx * 512:
                                       et * T + tb + sx * 512 + 512]
                        K.inc(v.tensor_tensor(xs, xs, pw, op=ALU.add), "v", 1)
                        evac[widx] = K.cnt["v"]
                    self.vbody(v_f)
                    widx += 1
        self._wo_used_t[slot] = K.cnt["t"]
        self._outproj_read_t = K.cnt["t"]
        xupd.append(K.cnt["v"])

    def ffn_sublayer(self, final):
        K = self
        nc = self.nc
        self.sublayer_barrier()
        if self.dbg and not getattr(self, "_dumped_x", False):
            self._dumped_x = True
            self.dump(getattr(self, "d_x", None), self.x_sb[:],
                      vwait=self.x_update_v)
        with contextlib.ExitStack() as stk:
            A = stk.enter_context
            sb = nc.sbuf_tensor
            wdt = BF16 if Builder.FFN_BF16_W else FP8
            wfc8 = A(sb(K.nm("f_wfc8"), [128, NCH * HFF], FP8))
            wpj8 = A(sb(K.nm("f_wpj8"), [128, HCH * 1024], wdt))
            ybuf = A(sb(K.nm("f_y"), [128, 2 * NCH * 512], BF16))
            hbuf = A(sb(K.nm("f_h"), [128, HCH * 512], BF16))
            sqb = A(sb(K.nm("f_sq"), [128, 2 * 512], BF16))
            ost = A(sb(K.nm("f_ost"), [128, 2 * 512], F32))

            if Builder.FFN_BF16_W:
                def syw(sy):
                    K.wait("sync", sy, "dg", self.quant_dg)
                    for c in range(NCH):
                        K.inc(sy.dma_start(
                            out=wfc8[:, c * HFF:(c + 1) * HFF],
                            in_=self.wfc8h[c * 128:(c + 1) * 128, :]), "ds", 16)
                self.ybody(syw)

                def syw2(g):
                    K.wait("gpsimd", g, "dg", self.quant_dg)
                    for c4 in range(4):
                        K.inc(g.dma_start(
                            out=wpj8[:, c4 * 8 * 1024:(c4 + 1) * 8 * 1024]
                            .rearrange("p (c e) -> p c e", e=1024),
                            in_=self.wproj8h.rearrange("(c p) e -> p c e",
                                                       p=128)
                            [:, c4 * 8:(c4 + 1) * 8, :]), "dg", 16)
                self.gbody(syw2)
                w8_ds = ("ds", K.cnt["ds"])
                self._wpj_ds = ("dg", K.cnt["dg"])
            else:
                def syw(g):
                    K.wait("gpsimd", g, "dg", K.cnt["dg"])
                    K.wait("gpsimd", g, "dg", self.quant_dg)
                    for c in range(NCH):
                        K.inc(g.dma_start(
                            out=wfc8[:, c * HFF:(c + 1) * HFF],
                            in_=self.wfc8h[c * 128:(c + 1) * 128, :]), "dg", 16)
                    for c4 in range(4):
                        K.inc(g.dma_start(
                            out=wpj8[:, c4 * 8 * 1024:(c4 + 1) * 8 * 1024]
                            .rearrange("p (c e) -> p c e", e=1024),
                            in_=self.wproj8h.rearrange("(c p) e -> p c e",
                                                       p=128)
                            [:, c4 * 8:(c4 + 1) * 8, :]), "dg", 16)
                self.gbody(syw)
                fl = self.dma_flush()
                w8_ds = ("dg", fl)
                self._wpj_ds = ("dg", fl)

            ps_fc = [A(nc.psum_tensor(K.nm(f"f_fc{i}"), [128, 512], F32))
                     for i in range(2)]
            ps_pj = [A(nc.psum_tensor(K.nm(f"f_pj{i}"), [128, 512], F32))
                     for i in range(2)]
            ps_sh = A(nc.psum_tensor(K.nm("f_sh"), [128, 512], F32))
            ps_shh = A(nc.psum_tensor(K.nm("f_shh"), [128, 512], F32))
            ps_r2 = A(nc.psum_tensor(K.nm("f_r2"), [128, 512], F32))
            ps_m2 = A(nc.psum_tensor(K.nm("f_m2"), [128, 512], F32))

            gelu_s = {}
            sq_v = {}
            fc_t_of = {}
            fc_last_t = {}
            pj_evac = {}
            xupd = []
            for s_ in range(8):
                self._ffn_strip(final, s_, wfc8, wpj8, ybuf, hbuf, sqb, ost,
                                ps_fc, ps_pj, ps_sh, ps_shh, ps_r2, ps_m2,
                                w8_ds, gelu_s, sq_v, fc_t_of, fc_last_t,
                                pj_evac, xupd)
            if not final:
                self.x_update_v = self.cnt["v"]

    def _ffn_strip(self, final, s_, wfc8, wpj8, ybuf, hbuf, sqb, ost,
                   ps_fc, ps_pj, ps_sh, ps_shh, ps_r2, ps_m2, w8_ds,
                   gelu_s, sq_v, fc_t_of, fc_last_t, pj_evac, xupd):
        K = self
        t0 = s_ * 512
        ss = s_ % 2
        rb = K.rowbuf
        eps = np.float32(1e-5)

        twaits = []
        if s_ >= 2:
            twaits.append(("t", fc_last_t[s_ - 2]))
        if s_ >= 1:
            twaits.append(("v", pj_evac[s_ - 1]))
        if Builder.FFN_SERIAL:
            self.sublayer_barrier()
        if Builder.FFN_LN_PSUM_ALT:
            lnps = (ps_r2, ps_m2, ps_r2[:], ps_m2[:])
        else:
            lnps = (ps_pj[0], ps_pj[1], ps_pj[0][:], ps_pj[1][:])
        ln_v = self.ln_strip(
            t0,
            lambda c, ss=ss: ybuf[:, ss * NCH * 512 + c * 512:
                                  ss * NCH * 512 + (c + 1) * 512],
            *lnps,
            xwaits=self._x_waits(), twaits=twaits, half=True)

        if Builder.FFN_SERIAL:
            self.sublayer_barrier()
        nostats = Builder.FFN_NO_STATS or Builder.FFN_NO_LN2
        for h in range(HCH):
            def t_fc(t, h=h):
                if h == 0:
                    K.wait("tensor", t, w8_ds[0], w8_ds[1])
                    K.wait("tensor", t, "v", ln_v)
                if h >= 2:
                    K.wait("tensor", t, "s", gelu_s[(s_, h - 2)])
                elif s_ >= 1:
                    K.wait("tensor", t, "s", gelu_s[(s_ - 1, HCH - 2 + h)])
                for c in range(NCH):
                    i = t.matmul(ps_fc[h % 2][:],
                                 wfc8[:, c * HFF + h * 128:
                                      c * HFF + h * 128 + 128],
                                 ybuf[:, (s_ % 2) * NCH * 512 + c * 512:
                                      (s_ % 2) * NCH * 512 + (c + 1) * 512],
                                 start=(c == 0), stop=(c == NCH - 1))
                if h >= 1 and not nostats:
                    K.wait("tensor", t, "s", gelu_s[(s_, h - 1)])
                    K.wait("tensor", t, "v", sq_v[(s_, h - 1)])
                    t.matmul(ps_sh[0:1, :], self.ones_bf[:],
                             hbuf[:, (h - 1) * 512: h * 512],
                             start=(h == 1), stop=False)
                    t.matmul(ps_shh[0:1, :], self.ones_bf[:],
                             sqb[:, ((h - 1) % 2) * 512:
                                 ((h - 1) % 2 + 1) * 512],
                             start=(h == 1), stop=False)
                K.inc(i, "t", 1)
            self.tbody(t_fc)
            fc_t = K.cnt["t"]
            fc_t_of[(s_, h)] = fc_t
            if h == HCH - 1:
                fc_last_t[s_] = fc_t

            def s_g(s, h=h, fc_t=fc_t):
                K.wait("scalar", s, "t", fc_t)
                K.inc(s.activation(hbuf[:, h * 512:(h + 1) * 512],
                                   ps_fc[h % 2][:], AF.Gelu), "s", 1)
            self.sbody(s_g)
            gelu_s[(s_, h)] = K.cnt["s"]

            def v_q(v, h=h, fc_t=fc_t):
                K.wait("vector", v, "s", gelu_s[(s_, h)])
                if h >= 2:
                    K.wait("vector", v, "t", fc_t_of[(s_, h - 1)])
                hs = hbuf[:, h * 512:(h + 1) * 512]
                K.inc(v.tensor_tensor(sqb[:, (h % 2) * 512:(h % 2 + 1) * 512],
                                      hs, hs, op=ALU.mult), "v", 1)
            self.vbody(v_q)
            sq_v[(s_, h)] = K.cnt["v"]

        def t_st(t):
            K.wait("tensor", t, "s", gelu_s[(s_, HCH - 1)])
            K.wait("tensor", t, "v", sq_v[(s_, HCH - 1)])
            if nostats:
                K.inc(t.matmul(ps_sh[0:1, :], self.ones_bf[:],
                               hbuf[:, 0:512], start=True, stop=True), "t", 1)
                return
            t.matmul(ps_sh[0:1, :], self.ones_bf[:],
                     hbuf[:, (HCH - 1) * 512: HCH * 512],
                     start=False, stop=True)
            i = t.matmul(ps_shh[0:1, :], self.ones_bf[:],
                         sqb[:, ((HCH - 1) % 2) * 512:
                             ((HCH - 1) % 2 + 1) * 512],
                         start=False, stop=True)
            K.inc(i, "t", 1)
        self.tbody(t_st)
        st_t = K.cnt["t"]

        if s_ == 0 and self.dbg and not getattr(self, "_dumped_ffn", False):
            self._dumped_ffn = True
            self.dump(getattr(self, "d_yf", None),
                      ybuf[:, 0:NCH * 512], twait=fc_last_t[s_])
            gcount = self.dump(getattr(self, "d_h", None), hbuf[:],
                               twait=st_t)
            self._h_guard_dg = gcount

            def bar_v(v):
                K.wait("vector", v, "dg", gcount)

            def bar_s(s):
                K.wait("scalar", s, "dg", gcount)

            def bar_t(t):
                K.wait("tensor", t, "dg", gcount)
            self.vbody(bar_v)
            self.sbody(bar_s)
            self.tbody(bar_t)

        if Builder.FFN_NO_LN2:
            def t_bx(t):
                K.wait("tensor", t, "t", st_t) if False else None
                t.matmul(ps_r2[:], self.ones_f[0:1, :], rb[0:1, 0:512],
                         start=True, stop=True)
                K.inc(t.matmul(ps_m2[:], self.ones_f[0:1, :],
                               rb[0:1, 0:512], start=True, stop=True), "t", 1)
            self.tbody(t_bx)
            b2_t = K.cnt["t"]
            z_v = K.cnt["v"]
            self._ffn_proj(final, s_, wpj8, hbuf, ybuf, ost, ps_pj, z_v,
                           pj_evac, xupd, t0)
            pj_evac[s_] = K.cnt["v"]
            return

        def v_v(v):
            K.wait("vector", v, "t", st_t)
            K.inc(v.tensor_scalar_mul(rb[0:1, 0:512], ps_sh[0:1, :],
                                      1.0 / HFF), "v", 1)
            K.inc(v.tensor_tensor(rb[0:1, 512:1024], rb[0:1, 0:512],
                                  rb[0:1, 0:512], op=ALU.mult), "v", 1)
            K.inc(v.scalar_tensor_tensor(rb[0:1, 512:1024], ps_shh[0:1, :],
                                         1.0 / HFF, rb[0:1, 512:1024],
                                         op0=ALU.mult, op1=ALU.subtract),
                  "v", 1)
        self.vbody(v_v)
        var_v = K.cnt["v"]

        def s_r(s):
            K.wait("scalar", s, "v", var_v)
            K.inc(s.activation(rb[0:1, 512:1024], rb[0:1, 512:1024],
                               AF.Ln, bias=eps), "s", 1)
            K.inc(s.activation(rb[0:1, 1024:1536], rb[0:1, 512:1024],
                               AF.Exp, scale=np.float32(-0.5)), "s", 1)
        self.sbody(s_r)
        rstd_s = K.cnt["s"]

        def v_m2(v):
            K.wait("vector", v, "s", rstd_s)
            K.inc(v.tensor_tensor(rb[0:1, 0:512], rb[0:1, 0:512],
                                  rb[0:1, 1024:1536], op=ALU.mult), "v", 1)
        self.vbody(v_m2)
        mur_v = K.cnt["v"]

        def t_b2(t):
            K.wait("tensor", t, "v", mur_v)
            t.matmul(ps_r2[:], self.ones_f[0:1, :], rb[0:1, 1024:1536],
                     start=True, stop=True)
            K.inc(t.matmul(ps_m2[:], self.ones_f[0:1, :], rb[0:1, 0:512],
                           start=True, stop=True), "t", 1)
        self.tbody(t_b2)
        b2_t = K.cnt["t"]

        for h in range(HCH if not Builder.FFN_NO_LN2 else 0):
            def v_z(v, h=h):
                if h == 0:
                    K.wait("vector", v, "t", b2_t)
                    K.wait("vector", v, "t", st_t)
                    K.wait("vector", v, "dg", getattr(self, "_h_guard_dg", 0))
                hs = hbuf[:, h * 512:(h + 1) * 512]
                K.inc(v.tensor_tensor(K.rowbuf[:, 512:1024], hs, ps_r2[:],
                                      op=ALU.mult), "v", 1)
                K.inc(v.scalar_tensor_tensor(hs, ps_m2[:], -1.0,
                                             K.rowbuf[:, 512:1024],
                                             op0=ALU.mult, op1=ALU.add),
                      "v", 1)
                K.inc(v.tensor_scalar(hs, hs, self.lnwb_sb[:, h:h + 1],
                                      self.lnwb_sb[:, 32 + h:32 + h + 1],
                                      op0=ALU.mult, op1=ALU.add), "v", 1)
            self.vbody(v_z)
        z_v = K.cnt["v"]

        if Builder.FFN_SERIAL:
            self.sublayer_barrier()
        self._ffn_proj(final, s_, wpj8, hbuf, ybuf, ost, ps_pj, z_v,
                       pj_evac, xupd, t0)
        pj_evac[s_] = K.cnt["v"]

    def _ffn_proj(self, final, s_, wpj8, hbuf, ybuf, ost, ps_pj, z_v,
                  pj_evac, xupd, t0):
        K = self
        wpj_ds = getattr(self, "_wpj_ds", 0)
        if not isinstance(wpj_ds, tuple):
            wpj_ds = ("ds", wpj_ds)
        for dt_ in range(NCH):
            def t_pj(t, dt_=dt_):
                if dt_ == 0:
                    K.wait("tensor", t, "v", z_v)
                    K.wait("tensor", t, wpj_ds[0], wpj_ds[1])
                if dt_ >= 2:
                    K.wait("tensor", t, "v", pj_evac[(s_, dt_ - 2)])
                for hc in range(HCH):
                    i = t.matmul(ps_pj[dt_ % 2][:],
                                 wpj8[:, hc * 1024 + dt_ * 128:
                                      hc * 1024 + dt_ * 128 + 128],
                                 hbuf[:, hc * 512:(hc + 1) * 512],
                                 start=(hc == 0), stop=(hc == HCH - 1))
                K.inc(i, "t", 1)
            self.tbody(t_pj)
            pj_t = K.cnt["t"]

            if not final:
                def v_e(v, dt_=dt_, pj_t=pj_t):
                    K.wait("vector", v, "t", pj_t)
                    xs = self.x_sb[:, dt_ * T + t0: dt_ * T + t0 + 512]
                    K.inc(v.tensor_tensor(xs, xs, ps_pj[dt_ % 2][:],
                                          op=ALU.add), "v", 1)
                    pj_evac[(s_, dt_)] = K.cnt["v"]
                self.vbody(v_e)
            else:
                def v_e(v, dt_=dt_, pj_t=pj_t):
                    K.wait("vector", v, "t", pj_t)
                    K.wait("vector", v, "ds",
                           self._ost_ds.get((dt_ + 1) % 2, 0)
                           if hasattr(self, "_ost_ds") else 0)
                    xs = self.x_sb[:, dt_ * T + t0: dt_ * T + t0 + 512]
                    K.inc(v.tensor_tensor(ost[:, (dt_ % 2) * 512:
                                              (dt_ % 2 + 1) * 512],
                                          xs, ps_pj[dt_ % 2][:], op=ALU.add),
                          "v", 1)
                    pj_evac[(s_, dt_)] = K.cnt["v"]
                    K.inc(v.memset(self.drain_sb[:], 0.0), "v", 1)
                self.vbody(v_e)
                ev = K.cnt["v"]

                def sy_o(sy, dt_=dt_, ev=ev):
                    K.wait("sync", sy, "v", ev)
                    K.inc(sy.dma_start(
                        out=self.out[dt_ * 128:(dt_ + 1) * 128, t0:t0 + 512],
                        in_=ost[:, (dt_ % 2) * 512:(dt_ % 2 + 1) * 512]),
                        "ds", 16)
                self.ybody(sy_o)
                if not hasattr(self, "_ost_ds"):
                    self._ost_ds = {}
                self._ost_ds[dt_ % 2] = K.cnt["ds"]

    def phase_final(self, direct_out):
        K = self
        nc = self.nc
        self.sublayer_barrier()
        if not direct_out:
            with contextlib.ExitStack() as stk:
                ost = stk.enter_context(
                    nc.sbuf_tensor(K.nm("fin_ost"), [128, 2 * 512], F32))
                prev_ds = {}
                for c in range(NCH):
                    for s_ in range(8):
                        idx = c * 8 + s_

                        def v_f(v, c=c, s_=s_, idx=idx):
                            if idx == 0:
                                for sem, val in self._x_waits():
                                    K.wait("vector", v, sem, val)
                            if idx >= 2:
                                K.wait("vector", v, "ds", prev_ds[idx - 1])
                            K.inc(v.tensor_copy(
                                ost[:, (idx % 2) * 512:(idx % 2 + 1) * 512],
                                self.x_sb[:, c * T + s_ * 512:
                                          c * T + s_ * 512 + 512]), "v", 1)
                            K.inc(v.memset(self.drain_sb[:], 0.0), "v", 1)
                        self.vbody(v_f)
                        cv = K.cnt["v"]

                        def sy_f(sy, c=c, s_=s_, idx=idx, cv=cv):
                            K.wait("sync", sy, "v", cv)
                            K.inc(sy.dma_start(
                                out=self.out[c * 128:(c + 1) * 128,
                                             s_ * 512:(s_ + 1) * 512],
                                in_=ost[:, (idx % 2) * 512:(idx % 2 + 1) * 512]),
                                "ds", 16)
                        self.ybody(sy_f)
                        prev_ds[idx] = K.cnt["ds"]

        ds_final = K.cnt["ds"]
        dg_final = K.cnt["dg"]

        def g_fin(g):
            K.wait("gpsimd", g, "ds", ds_final)
            K.wait("gpsimd", g, "dg", dg_final)
        self.gbody(g_fin)


_CACHED = {}


def _build_nc(nsub):
    if nsub not in _CACHED:
        _CACHED[nsub] = Builder(nsub).build()
    return _CACHED[nsub]


def _host_constants():
    half = 32
    inv = (1.0 / (10000.0 ** (np.arange(half, dtype=np.float64) / half)))
    ang = np.arange(L, dtype=np.float64)[:, None] * inv[None, :]
    cos = np.cos(ang).T.astype(np.float32)
    sin = np.sin(ang).T.astype(np.float32)
    cosrep = np.concatenate([cos, cos, cos, cos], axis=0)
    sinrep = np.concatenate([-sin, sin, -sin, sin], axis=0)
    p32 = np.zeros((128, 128), np.float32)
    for r in range(128):
        p32[r, r ^ 32] = 1.0
    diag = (np.arange(128)[:, None] <= np.arange(128)[None, :]).astype(
        np.float32)
    ident = np.eye(128, dtype=np.float32)
    p64m = np.zeros((128, 128), np.float32)
    for m in range(64, 128):
        p64m[m - 64, m] = 1.0
    return cosrep, sinrep, p32, diag, ident, p64m


def _perm_b():
    p = np.zeros(D, np.int64)
    for h in range(16):
        base = h * 64
        p[base:base + 32] = base + 2 * np.arange(32)
        p[base + 32:base + 64] = base + 2 * np.arange(32) + 1
    return p


def kernel(L0_attn_a_x, L0_attn_a_w_qkv, L0_attn_a_w_out,
           L0_attn_b_wq, L0_attn_b_wk, L0_attn_b_wv, L0_attn_b_wo,
           L0_ffn_W_fc, L0_ffn_ln_w, L0_ffn_ln_b, L0_ffn_W_proj):
    x = np.ascontiguousarray(np.asarray(L0_attn_a_x, np.float32))
    cosrep, sinrep, p32, diag, ident, p64m = _host_constants()
    pb = _perm_b()

    wqkvT = np.ascontiguousarray(np.asarray(L0_attn_a_w_qkv, np.float32).T)
    woutT = np.ascontiguousarray(np.asarray(L0_attn_a_w_out, np.float32).T)
    wqT = np.ascontiguousarray(np.asarray(L0_attn_b_wq, np.float32).T[:, pb])
    wkT = np.ascontiguousarray(np.asarray(L0_attn_b_wk, np.float32).T[:, pb])
    wvT = np.ascontiguousarray(np.asarray(L0_attn_b_wv, np.float32).T)
    woT = np.ascontiguousarray(np.asarray(L0_attn_b_wo, np.float32).T)
    wfcT = np.ascontiguousarray(np.asarray(L0_ffn_W_fc, np.float32).T)
    wprojT = np.ascontiguousarray(np.asarray(L0_ffn_W_proj, np.float32).T)
    lnw = np.asarray(L0_ffn_ln_w, np.float32)
    lnb = np.asarray(L0_ffn_ln_b, np.float32)
    lnwb = np.zeros((128, 64), np.float32)
    lnwb[:, 0:32] = lnw.reshape(32, 128).T
    lnwb[:, 32:64] = lnb.reshape(32, 128).T

    nc = _build_nc(N_SUBLAYERS)
    shared = dict(wqkvT=wqkvT, woutT=woutT, wqT=wqT, wkT=wkT, wvT=wvT,
                  woT=woT, wfcT=wfcT, wprojT=wprojT, lnwb=lnwb,
                  cosrep=cosrep, sinrep=sinrep, p32=p32, diagm=diag,
                  ident=ident, p64m=p64m)
    in_maps = []
    for core in range(8):
        xs = x[2 * core:2 * core + 2]
        xTc = np.ascontiguousarray(
            xs.reshape(T, D).T)
        m = dict(shared)
        m["xT"] = xTc
        in_maps.append(m)

    res = run_bass_kernel_spmd(nc, in_maps, core_ids=list(range(8)))
    outs = []
    for core in range(8):
        o = res.results[core]["out"]
        outs.append(np.ascontiguousarray(o.T).reshape(BLOC, L, D))
    return np.concatenate(outs, axis=0)
